# version 18
# baseline (speedup 1.0000x reference)
import os
import numpy as np
from contextlib import ExitStack

import concourse.bass as bass
import concourse.mybir as mybir
from concourse.bass_utils import run_bass_kernel_spmd

F32 = mybir.dt.float32
BF16 = mybir.dt.bfloat16
I32 = mybir.dt.int32
AF = mybir.ActivationFunctionType
ALU = mybir.AluOpType
AX = mybir.AxisListType

T = 1024
TH = 512
C = 2048
CK = 16
II = 8192
IK = 64
H = 16
NC = 8
EPS_Q = 1e-5
EPS_LN = 1e-5
EPS_NRM = 1e-12

SEMS = ['bar', 'pe', 'act', 'dve', 'pool',
        'lA', 'lB', 'lC', 'lD', 'lE', 'lF', 'lG', 'lH', 'lI', 'lJ',
        'tq', 'sA', 'sB', 'sC', 'sD', 'cc']


class Buf:
    __slots__ = ('w', 'r', 'const')

    def __init__(self, const=False):
        self.w = None
        self.r = {}
        self.const = const


class Prog:
    def __init__(self):
        self.streams = {k: [] for k in ('sync', 'scalar', 'vector', 'tensor', 'gpsimd')}
        self.cnt = {s: 0 for s in SEMS}
        self.waited = {}
        self.B = {}

    def buf(self, name, const=False):
        if name not in self.B:
            self.B[name] = Buf(const)
        return self.B[name]

    def _waits(self, engk, evs):
        out = []
        for ev in evs:
            if ev is None:
                continue
            s, v = ev
            key = (engk, s)
            if v <= self.waited.get(key, 0):
                continue
            self.waited[key] = v
            out.append((s, v))
        return out

    def op(self, engk, sem, reads, writes, build, amt=1, group=None, extra_evs=()):
        evs = list(extra_evs)
        if amt == 16:
            evs.append((sem, self.cnt[sem]))
        for b in reads:
            evs.append(b.w)
        for b in writes:
            evs.append(b.w)
            evs.extend(b.r.values())
        wl = self._waits(engk, evs)
        self.cnt[sem] += amt
        ev = (sem, self.cnt[sem])
        self.streams[engk].append((wl, group if group is not None else [build], sem, amt))
        for b in reads:
            if not b.const:
                b.r[engk] = ev
        for b in writes:
            b.w = ev
            b.r = {}
        return ev

    def realize(self, eng, engk, sems):
        for wl, builders, sem, amt in self.streams[engk]:
            for s, v in wl:
                eng.wait_ge(sems[s], v)
            ins = None
            for b in builders:
                ins = b(eng)
            ins.then_inc(sems[sem], amt)


def build_nc():
    nc = bass.Bass("TRN2", target_bir_lowering=False)
    p = Prog()
    buf = p.buf

    def inp(name, shape, dt=F32):
        return nc.dram_tensor(name, shape, dt, kind="ExternalInput").ap()

    x_ext = inp("x_fm", [C, T])
    wg_r = inp("wg_r", [IK, 128, CK, 128])
    wu_r = inp("wu_r", [IK, 128, CK, 128])
    wd_r = inp("wd_r", [CK, 8, 128, 8, 128])
    wk_r = inp("wk_r", [CK, 128, CK, 128])
    wv_r = inp("wv_r", [CK, 128, CK, 128])
    wq_r = inp("wq_r", [CK, 128, CK, 128])
    wgt_r = inp("wgt_r", [CK, 128, CK, 128])
    wo_r = inp("wo_r", [CK, 128, CK, 128])
    wg_sl = inp("wg_sl", [8, 128, CK, 128])
    wu_sl = inp("wu_sl", [8, 128, CK, 128])
    wd_sl = inp("wd_sl", [2, 8, 128, 8, 128])
    convw = inp("convw", [128, 64])
    convb = inp("convb", [128, CK])
    gateb = inp("gateb", [128, CK])
    lnw = inp("lnw", [128, CK])
    lnb = inp("lnb", [128, CK])
    gamma_row = inp("gamma_row", [1, H])
    sel = inp("sel", [NC, 1])
    ones_row = inp("ones_row", [1, 128])
    ones_col = inp("ones_col", [128, 1])
    ones_colb = inp("ones_colb", [128, 1], BF16)
    ident = inp("ident", [128, 128])
    iota_t = inp("iota_t", [128, T])

    y_ext = nc.dram_tensor("y_fm", [C, T], F32, kind="ExternalOutput").ap()

    h_dram = nc.dram_tensor("h_dram", [IK, 128, T], F32).ap()
    wgq_dram = nc.dram_tensor("wgq_dram", [IK, 128, C], BF16).ap()
    wuq_dram = nc.dram_tensor("wuq_dram", [IK, 128, C], BF16).ap()
    wdq_dram = nc.dram_tensor("wdq_dram", [CK, 8, 128, 1024], BF16).ap()
    _dbg = {"kind": "ExternalOutput"} if os.environ.get("KD_DEBUG_SCRATCH") else {}
    res1_dram = nc.dram_tensor("res1_dram", [CK, 128, T], F32, **_dbg).ap()
    res2_dram = nc.dram_tensor("res2_dram", [CK, 128, T], F32, **_dbg).ap()
    xcb_dram = nc.dram_tensor("xcb_dram", [CK, 128, T], BF16).ap()
    s2_dram = nc.dram_tensor("s2_dram", [CK, 128, T], F32).ap()
    cc_in = nc.dram_tensor("cc_in", [1, 8192], F32).ap()
    cc_out = nc.dram_tensor("cc_out", [NC, 8192], F32).ap()
    bnc = nc.dram_tensor("bnc", [1, 8192], F32).ap()

    with ExitStack() as root:
        E = root.enter_context
        sems = {s: E(nc.semaphore(s)) for s in SEMS}

        s_ones_row = E(nc.sbuf_tensor("c_ones_row", [1, 128], F32))
        s_ones_col = E(nc.sbuf_tensor("c_ones_col", [128, 1], F32))
        s_ones_colb = E(nc.sbuf_tensor("c_ones_colb", [128, 1], BF16))
        s_ident = E(nc.sbuf_tensor("c_ident", [128, 128], F32))
        s_iota = E(nc.sbuf_tensor("c_iota", [128, T], F32))
        s_convw = E(nc.sbuf_tensor("c_convw", [128, 64], F32))
        s_convb = E(nc.sbuf_tensor("c_convb", [128, CK], F32))
        s_gateb = E(nc.sbuf_tensor("c_gateb", [128, CK], F32))
        s_lnw = E(nc.sbuf_tensor("c_lnw", [128, CK], F32))
        s_lnb = E(nc.sbuf_tensor("c_lnb", [128, CK], F32))
        s_sel = E(nc.sbuf_tensor("c_sel", [NC, 1], F32))
        s_grow = E(nc.sbuf_tensor("c_grow", [1, H], F32))
        s_lngrow = E(nc.sbuf_tensor("c_lngrow", [1, H], F32))
        s_lnm = E(nc.sbuf_tensor("c_lnm", [128, H], F32))
        s_gcols = E(nc.sbuf_tensor("c_gcols", [128, H], F32))
        s_wsrow = E(nc.sbuf_tensor("c_wsrow", [1, 3], F32))
        s_rwsrow = E(nc.sbuf_tensor("c_rwsrow", [1, 3], F32))
        s_wscols = E(nc.sbuf_tensor("c_wscols", [128, 3], F32))
        s_rwscols = E(nc.sbuf_tensor("c_rwscols", [128, 3], F32))
        s_stage16 = E(nc.sbuf_tensor("c_stage16", [128, 16], F32))
        s_stage48 = E(nc.sbuf_tensor("c_stage48", [128, 48], F32))
        s_evac16 = E(nc.sbuf_tensor("c_evac16", [16, 128], F32))
        s_evac48 = E(nc.sbuf_tensor("c_evac48", [48, 128], F32))
        s_actb = E(nc.sbuf_tensor("c_actb", [128, CK, T], BF16))
        s_rowM = E(nc.sbuf_tensor("c_rowM", [1, T], F32))
        s_rowE = E(nc.sbuf_tensor("c_rowE", [1, T], F32))
        s_rowT = E(nc.sbuf_tensor("c_rowT", [1, T], F32))
        s_rowV = E(nc.sbuf_tensor("c_rowV", [1, T], F32))
        s_rowR = E(nc.sbuf_tensor("c_rowR", [1, T], F32))
        s_bcA = E(nc.sbuf_tensor("c_bcA", [128, T], F32))
        s_bcB = E(nc.sbuf_tensor("c_bcB", [128, T], F32))
        s_sin = E(nc.sbuf_tensor("c_sin", [128, H], F32))
        s_tailc = E(nc.sbuf_tensor("c_tailc", [128, 48], F32))
        s_head6 = E(nc.sbuf_tensor("c_head6", [128, CK * 6], F32))
        s_epsln = E(nc.sbuf_tensor("c_epsln", [128, 1], F32))

        pb = [E(nc.psum_tensor(f"pb{k}", [128, 512], F32)) for k in range(8)]
        PB = [buf(f"pb{k}") for k in range(8)]
        CONSTS = buf("consts", const=True)

        def dma_in(sem, dst_ap, src_ap, wbufs, rbufs=(), extra_evs=()):
            return p.op('sync', sem, list(rbufs), list(wbufs),
                        lambda e, d=dst_ap, s=src_ap: e.dma_start(out=d, in_=s),
                        amt=16, extra_evs=extra_evs)

        def dma_out(sem, dst_ap, src_ap, rbufs, wbufs=(), extra_evs=()):
            return p.op('gpsimd', sem, list(rbufs), list(wbufs),
                        lambda e, d=dst_ap, s=src_ap: e.dma_start(out=d, in_=s),
                        amt=16, extra_evs=extra_evs)

        def act(reads, writes, build, extra_evs=()):
            return p.op('scalar', 'act', reads, writes, build, extra_evs=extra_evs)

        def dve(reads, writes, build, extra_evs=()):
            return p.op('vector', 'dve', reads, writes, build, extra_evs=extra_evs)

        def pool(reads, writes, build, extra_evs=()):
            return p.op('gpsimd', 'pool', reads, writes, build, extra_evs=extra_evs)

        def mm(reads, writes, builders, extra_evs=()):
            return p.op('tensor', 'pe', reads, writes, None, group=builders,
                        extra_evs=extra_evs)

        def tdma(reads, writes, build, extra_evs=()):
            return p.op('scalar', 'tq', reads, writes, build, amt=16, extra_evs=extra_evs)

        def barrier():
            evs = [(s2, p.cnt[s2]) for s2 in SEMS if p.cnt[s2] > 0]
            for engk in ('sync', 'scalar', 'vector', 'tensor', 'gpsimd'):
                wl = p._waits(engk, evs)
                p.cnt['bar'] += 1
                p.streams[engk].append((wl, [lambda e: e.nop()], 'bar', 1))

        def bcast_half(row_ap, bank, rowbuf):
            return mm([CONSTS, rowbuf], [PB[bank]],
                      [lambda e, r=row_ap, k=bank:
                       e.matmul(pb[k][:, :], s_ones_row[:], r, start=True, stop=True)])

        def bcast_pair(row_tensor, b0, b1, rowbuf, dst_tensor, dstbuf):
            bcast_half(row_tensor[0:1, 0:TH], b0, rowbuf)
            bcast_half(row_tensor[0:1, TH:T], b1, rowbuf)
            act([PB[b0]], [dstbuf],
                lambda e, k=b0: e.activation(out=dst_tensor[:, 0:TH], in_=pb[k][:, :], func=AF.Copy))
            act([PB[b1], dstbuf], [dstbuf],
                lambda e, k=b1: e.activation(out=dst_tensor[:, TH:T], in_=pb[k][:, :], func=AF.Copy))

        def row_dance(stagebuf, bank, outbuf, dst_row):
            mm([stagebuf, CONSTS], [PB[bank]],
               [lambda e, k=bank: e.transpose(pb[k][0:16, 0:128], s_stage16[:], s_ident[:])])
            evb = buf("evac16")
            act([PB[bank]], [evb],
                lambda e, k=bank: e.activation(out=s_evac16[:], in_=pb[k][0:16, 0:128], func=AF.Copy))
            bncb = buf("bnc")
            tdma([evb], [bncb],
                 lambda e: e.dma_start(out=bnc.rearrange("a (r c) -> (a r) c", c=128)[0:16, :],
                                       in_=s_evac16[:]))
            tdma([bncb], [outbuf],
                 lambda e, d=dst_row: e.dma_start(out=d[:], in_=bnc[:, 0:2048]))

        def dance32(fold_tensor, foldbuf, outbuf, dst_row):
            stg = buf("stage16")
            for j in range(16):
                bk = 4 + (j % 2)
                mm([foldbuf, CONSTS], [PB[bk]],
                   [lambda e, k=bk, j2=j, ft=fold_tensor:
                    e.transpose(pb[k][:, 0:128], ft[:, j2 * 128:(j2 + 1) * 128], s_ident[:])])
                dve([PB[bk]], [stg],
                    lambda e, k=bk, j2=j: e.tensor_reduce(
                        out=s_stage16[:, j2:j2 + 1], in_=pb[k][:, 0:128], axis=AX.X,
                        op=(ALU.max if j2 < 8 else ALU.min)))
            row_dance(stg, 6, outbuf, dst_row)

        def ln_stat_rows(rm, rv, rrstd):
            act([PB[0]], [rm],
                lambda e: e.activation(out=s_rowM[0:1, 0:TH], in_=pb[0][0:1, :], func=AF.Copy,
                                       scale=1.0 / C))
            act([PB[1], rm], [rm],
                lambda e: e.activation(out=s_rowM[0:1, TH:T], in_=pb[1][0:1, :], func=AF.Copy,
                                       scale=1.0 / C))
            re_ = buf("rowE")
            act([PB[2]], [re_],
                lambda e: e.activation(out=s_rowE[0:1, 0:TH], in_=pb[2][0:1, :], func=AF.Copy,
                                       scale=1.0 / C))
            act([PB[3], re_], [re_],
                lambda e: e.activation(out=s_rowE[0:1, TH:T], in_=pb[3][0:1, :], func=AF.Copy,
                                       scale=1.0 / C))
            rt = buf("rowT")
            dve([rm], [rt],
                lambda e: e.scalar_tensor_tensor(out=s_rowT[:], in0=s_rowM[:], scalar=1.0,
                                                 in1=s_rowM[:], op0=ALU.mult, op1=ALU.mult))
            dve([re_, rt], [rv],
                lambda e: e.tensor_tensor(out=s_rowV[:], in0=s_rowE[:], in1=s_rowT[:],
                                          op=ALU.subtract))
            act([rv, CONSTS], [rt],
                lambda e: e.activation(out=s_rowT[:], in_=s_rowV[:], func=AF.Sqrt,
                                       bias=s_epsln[0:1, :]))
            dve([rt], [rrstd],
                lambda e: e.reciprocal(out=s_rowR[:], in_=s_rowT[:]))

        for dst, src in [(s_ones_row, ones_row), (s_ones_col, ones_col),
                         (s_ones_colb, ones_colb), (s_ident, ident),
                         (s_iota, iota_t), (s_convw, convw), (s_convb, convb),
                         (s_gateb, gateb), (s_lnw, lnw), (s_lnb, lnb), (s_sel, sel),
                         (s_grow, gamma_row)]:
            dma_in('lA', dst[:], src, [CONSTS])

        p.op('vector', 'dve', [], [CONSTS], lambda e: e.memset(s_epsln[:], EPS_LN))
        act([CONSTS], [CONSTS],
            lambda e: e.activation(out=s_lngrow[:], in_=s_grow[:], func=AF.Ln))
        mm([CONSTS], [PB[0]],
           [lambda e: e.matmul(pb[0][:, 0:H], s_ones_row[:], s_lngrow[:], start=True, stop=True)])
        act([PB[0]], [CONSTS],
            lambda e: e.activation(out=s_lnm[:], in_=pb[0][:, 0:H], func=AF.Copy))
        act([CONSTS], [CONSTS],
            lambda e: e.activation(out=s_gcols[:], in_=s_lnm[:], func=AF.Exp))

        def emit_scale_pass(E2):
                s_sl = [E2(nc.sbuf_tensor(f"p0_sl{j}", [128, 1024], F32)) for j in range(2)]
                s_par = [E2(nc.sbuf_tensor(f"p0_par{j}", [128, 1], F32)) for j in range(2)]
                s_acc = E2(nc.sbuf_tensor("p0_acc", [128, 3], F32))
                s_ccrow = E2(nc.sbuf_tensor("p0_ccrow", [1, 3], F32))
                slb = [buf(f"p0_sl{j}") for j in range(2)]
                parb = [buf(f"p0_par{j}") for j in range(2)]
                accb = buf("p0_acc")
                ccib = buf("d_ccin")
                ccob = buf("d_ccout")
                ccrb = buf("p0_ccrow")

                def slices_of(w_ap, widx):
                    if widx < 2:
                        return [(w_ap[j].rearrange("p ck q -> p (ck q)")[:, h * 1024:(h + 1) * 1024],
                                 1024) for j in range(8) for h in range(2)]
                    return [(w_ap[a, b].rearrange("p ib q -> p (ib q)"), 1024)
                            for a in range(2) for b in range(8)]

                for widx, w_ap in enumerate([wg_sl, wu_sl, wd_sl]):
                    srcs = slices_of(w_ap, widx)
                    for j, (sap, ncols) in enumerate(srcs):
                        par = j % 2
                        dma_in('lE' if par == 0 else 'lF', s_sl[par][:, 0:ncols], sap, [slb[par]])
                        dve([slb[par]], [parb[par]],
                            lambda e, pa=par, n=ncols: e.tensor_reduce(
                                out=s_par[pa][:], in_=s_sl[pa][:, 0:n], axis=AX.X, op=ALU.add,
                                apply_absolute_value=True))
                        if j == 0:
                            pool([parb[par]], [accb],
                                 lambda e, pa=par, w=widx: e.tensor_scalar(
                                     out=s_acc[:, w:w + 1], in0=s_par[pa][:], scalar1=1.0,
                                     scalar2=None, op0=ALU.mult))
                        else:
                            pool([parb[par], accb], [accb],
                                 lambda e, pa=par, w=widx: e.tensor_tensor(
                                     out=s_acc[:, w:w + 1], in0=s_acc[:, w:w + 1],
                                     in1=s_par[pa][:], op=ALU.add))
                mm([accb, CONSTS], [PB[1]],
                   [(lambda e, w=w: e.matmul(pb[1][0:1, w:w + 1], s_acc[:, w:w + 1],
                                             s_ones_col[:], start=True, stop=True)) for w in range(3)])
                act([PB[1]], [ccrb],
                    lambda e: e.activation(out=s_ccrow[:], in_=pb[1][0:1, 0:3], func=AF.Copy))
                dma_out('sA', cc_in[:, 0:3], s_ccrow[:], [ccrb], wbufs=[ccib])
                p.op('gpsimd', 'cc', [ccib], [ccob],
                     lambda e: e.collective_compute(
                         "AllReduce", ALU.add, replica_groups=[list(range(NC))],
                         ins=[cc_in[:, 0:3]], outs=[cc_out[0:1, 0:3]]))
                dma_in('lD', s_ccrow[:], cc_out[0:1, 0:3], [ccrb], rbufs=[ccob])
                wsb = buf("wsrow")
                rwsb = buf("rwsrow")
                act([ccrb], [wsb],
                    lambda e: e.activation(out=s_wsrow[:], in_=s_ccrow[:], func=AF.Copy,
                                           bias=EPS_Q, scale=1.0 / float(C * II)))
                dve([wsb], [rwsb],
                    lambda e: e.reciprocal(out=s_rwsrow[:], in_=s_wsrow[:]))
                mm([wsb, CONSTS], [PB[2]],
                   [lambda e: e.matmul(pb[2][:, 0:3], s_ones_row[:], s_wsrow[:], start=True, stop=True)])
                act([PB[2]], [buf("wscols")],
                    lambda e: e.activation(out=s_wscols[:], in_=pb[2][:, 0:3], func=AF.Copy))
                mm([rwsb, CONSTS], [PB[3]],
                   [lambda e: e.matmul(pb[3][:, 0:3], s_ones_row[:], s_rwsrow[:], start=True, stop=True)])
                act([PB[3]], [buf("rwscols")],
                    lambda e: e.activation(out=s_rwscols[:], in_=pb[3][:, 0:3], func=AF.Copy))


        ACTB = [buf(f"actb{ck}") for ck in range(CK)]
        BCA = buf("bcA")
        BCB = buf("bcB")
        XSBC = buf("xsbc")
        SROW = buf("srow")
        RM = buf("rowM")
        RV = buf("rowV")
        RR = buf("rowR")
        RA = buf("rowA")
        RE = buf("rowE")
        RT = buf("rowT")
        RMX = buf("rowMX")

        def emit_ffn(li):
            first = (li == 0)
            ffn_stack = ExitStack()
            s_srow = ffn_stack.enter_context(nc.sbuf_tensor(f"f{li}_srow", [1, T], F32))
            s_xsbc = ffn_stack.enter_context(nc.sbuf_tensor(f"f{li}_xsbc", [128, T], F32))
            s_rowA = ffn_stack.enter_context(nc.sbuf_tensor(f"f{li}_rowA", [1, T], F32))
            s_rowMX = ffn_stack.enter_context(nc.sbuf_tensor(f"f{li}_rowMX", [1, 2048], F32))

            def x_chunk_src(ck):
                return x_ext[ck * 128:(ck + 1) * 128, :] if first else res2_dram[ck]

            x_dep = (lambda ck: []) if first else (lambda ck: [buf(f"d_res2_{ck}")])

            barrier()
            with ExitStack() as sA:
                E2 = sA.enter_context
                s_x = E2(nc.sbuf_tensor(f"f{li}_x", [128, CK, T], F32))
                s_sq = [E2(nc.sbuf_tensor(f"f{li}_sq{j}", [128, T], F32)) for j in range(2)]
                s_fold = [E2(nc.sbuf_tensor(f"f{li}_fold{j}", [128, 2048], F32)) for j in range(2)]
                s_t1 = [E2(nc.sbuf_tensor(f"f{li}_t1{j}", [128, T], F32)) for j in range(2)]
                s_i32 = [E2(nc.sbuf_tensor(f"f{li}_i32{j}", [128, T], I32)) for j in range(2)]
                xb = [buf(f"f{li}_x{ck}") for ck in range(CK)]
                sqb = [buf(f"f{li}_sq{j}") for j in range(2)]
                foldb = [buf(f"f{li}_fold{j}") for j in range(2)]
                t1b = [buf(f"f{li}_t1{j}") for j in range(2)]
                i32b = [buf(f"f{li}_i32{j}") for j in range(2)]

                ld_evs = []
                for ck in range(CK):
                    ld_evs.append(dma_in('lA' if ck % 2 == 0 else 'lB', s_x[:, ck, :],
                                         x_chunk_src(ck), [xb[ck]], rbufs=x_dep(ck)))
                x_all = (ld_evs[-1], ld_evs[-2])

                if first:
                    emit_scale_pass(E2)

                for ck in range(CK):
                    par = ck % 2
                    act([xb[ck]], [sqb[par]],
                        lambda e, c=ck, pa=par: e.activation(out=s_sq[pa][:], in_=s_x[:, c, :],
                                                             func=AF.Square))
                    st, sp = (ck == 0), (ck == CK - 1)
                    mm([xb[ck], CONSTS], [PB[0], PB[1]],
                       [(lambda e, c=ck, st=st, sp=sp:
                         e.matmul(pb[0][0:1, :], s_ones_col[:], s_x[:, c, 0:TH], start=st, stop=sp)),
                        (lambda e, c=ck, st=st, sp=sp:
                         e.matmul(pb[1][0:1, :], s_ones_col[:], s_x[:, c, TH:T], start=st, stop=sp))])
                    mm([sqb[par], CONSTS], [PB[2], PB[3]],
                       [(lambda e, pa=par, st=st, sp=sp:
                         e.matmul(pb[2][0:1, :], s_ones_col[:], s_sq[pa][:, 0:TH], start=st, stop=sp)),
                        (lambda e, pa=par, st=st, sp=sp:
                         e.matmul(pb[3][0:1, :], s_ones_col[:], s_sq[pa][:, TH:T], start=st, stop=sp))])
                    if ck == 0:
                        pool([xb[ck]], [foldb[0]],
                             lambda e, c=ck: e.tensor_scalar(
                                 out=s_fold[0][:, 0:1024], in0=s_x[:, c, :], scalar1=1.0,
                                 scalar2=None, op0=ALU.mult))
                        pool([xb[ck], foldb[0]], [foldb[0]],
                             lambda e, c=ck: e.tensor_scalar(
                                 out=s_fold[0][:, 1024:2048], in0=s_x[:, c, :], scalar1=1.0,
                                 scalar2=None, op0=ALU.mult))
                    else:
                        sfs, sfd = s_fold[(ck + 1) % 2], s_fold[ck % 2]
                        dve([xb[ck], foldb[(ck + 1) % 2]], [foldb[ck % 2]],
                            lambda e, c=ck, a=sfs, b=sfd: e.tensor_tensor(
                                out=b[:, 0:1024], in0=a[:, 0:1024], in1=s_x[:, c, :], op=ALU.max),)
                        dve([xb[ck], foldb[(ck + 1) % 2], foldb[ck % 2]], [foldb[ck % 2]],
                            lambda e, c=ck, a=sfs, b=sfd: e.tensor_tensor(
                                out=b[:, 1024:2048], in0=a[:, 1024:2048], in1=s_x[:, c, :], op=ALU.min))
                fin, sfin = foldb[(CK - 1) % 2], s_fold[(CK - 1) % 2]
                dance32(sfin, fin, RMX, s_rowMX)
                ln_stat_rows(RM, RV, RR)
                dve([RMX, RM], [RA],
                    lambda e: e.tensor_tensor(out=s_rowA[:], in0=s_rowMX[0:1, 0:T],
                                              in1=s_rowM[:], op=ALU.subtract))
                dve([RMX, RM], [RT],
                    lambda e: e.tensor_tensor(out=s_rowT[:], in0=s_rowM[:],
                                              in1=s_rowMX[0:1, T:2 * T], op=ALU.subtract))
                dve([RA, RT], [RE],
                    lambda e: e.tensor_tensor(out=s_rowE[:], in0=s_rowA[:], in1=s_rowT[:],
                                              op=ALU.max))
                dve([RE, RR], [RA],
                    lambda e: e.tensor_tensor(out=s_rowA[:], in0=s_rowE[:], in1=s_rowR[:],
                                              op=ALU.mult))
                act([RA], [SROW],
                    lambda e: e.activation(out=s_srow[:], in_=s_rowA[:], func=AF.Copy,
                                           scale=1.0 / 127.0, bias=EPS_Q))
                dve([SROW], [RA],
                    lambda e: e.reciprocal(out=s_rowA[:], in_=s_srow[:]))
                dve([RR, RA], [RT],
                    lambda e: e.tensor_tensor(out=s_rowT[:], in0=s_rowR[:], in1=s_rowA[:],
                                              op=ALU.mult))
                dve([RM, RT], [RV],
                    lambda e: e.scalar_tensor_tensor(out=s_rowV[:], in0=s_rowM[:], scalar=-1.0,
                                                     in1=s_rowT[:], op0=ALU.mult, op1=ALU.mult))
                bcast_pair(s_rowT, 4, 5, RT, s_bcA, BCA)
                bcast_pair(s_rowV, 6, 7, RV, s_bcB, BCB)
                bcast_pair(s_srow, 4, 5, SROW, s_xsbc, XSBC)
                for ck in range(CK):
                    par = ck % 2
                    dve([xb[ck], BCA], [t1b[par]],
                        lambda e, c=ck, pa=par: e.tensor_tensor(
                            out=s_t1[pa][:], in0=s_x[:, c, :], in1=s_bcA[:], op=ALU.mult))
                    dve([t1b[par], BCB], [i32b[par]],
                        lambda e, pa=par: e.tensor_tensor(
                            out=s_i32[pa][:], in0=s_t1[pa][:], in1=s_bcB[:], op=ALU.add))
                    act([i32b[par]], [ACTB[ck]],
                        lambda e, c=ck, pa=par: e.activation(out=s_actb[:, c, :],
                                                             in_=s_i32[pa][:], func=AF.Copy))

            barrier()
            with ExitStack() as sB:
                E2 = sB.enter_context
                s_wf = [E2(nc.sbuf_tensor(f"f{li}b_wf{j}", [128, C], F32)) for j in range(4)]
                s_wi = [E2(nc.sbuf_tensor(f"f{li}b_wi{j}", [128, C], I32)) for j in range(1)]
                s_gq = [E2(nc.sbuf_tensor(f"f{li}b_gq{j}", [128, C], BF16)) for j in range(4)]
                s_uq = [E2(nc.sbuf_tensor(f"f{li}b_uq{j}", [128, C], BF16)) for j in range(4)]
                s_gr = [E2(nc.sbuf_tensor(f"f{li}b_gr{j}", [128, TH], F32)) for j in range(4)]
                s_gs = [E2(nc.sbuf_tensor(f"f{li}b_gs{j}", [128, TH], F32)) for j in range(4)]
                s_h = [E2(nc.sbuf_tensor(f"f{li}b_h{j}", [128, T], F32)) for j in range(2)]
                s_hf = [E2(nc.sbuf_tensor(f"f{li}b_hf{j}", [128, 2048], F32)) for j in range(2)]
                wfb = [buf(f"f{li}b_wf{j}") for j in range(4)]
                wib = [buf(f"f{li}b_wi{j}") for j in range(1)]
                gqb = [buf(f"f{li}b_gq{j}") for j in range(4)]
                uqb = [buf(f"f{li}b_uq{j}") for j in range(4)]
                grb = [buf(f"f{li}b_gr{j}") for j in range(4)]
                gsb = [buf(f"f{li}b_gs{j}") for j in range(4)]
                hb = [buf(f"f{li}b_h{j}") for j in range(2)]
                hfb = [buf(f"f{li}b_hf{j}") for j in range(2)]

                def load_q(i, which):
                    par = i % 4
                    qp = i % 2
                    wsrc_r = wg_r if which == 'g' else wu_r
                    wq_d = wgq_dram if which == 'g' else wuq_dram
                    dstT = s_gq[par] if which == 'g' else s_uq[par]
                    dstB = gqb[par] if which == 'g' else uqb[par]
                    semld = {'g': ('lA', 'lB', 'lC', 'lD'),
                             'u': ('lE', 'lF', 'lG', 'lH')}[which][par]
                    dscr = buf(f"d_{'wgq' if which == 'g' else 'wuq'}_{i}")
                    if first:
                        rwi = 0 if which == 'g' else 1
                        dma_in(semld, s_wf[par][:], wsrc_r[i].rearrange("p ck q -> p (ck q)"),
                               [wfb[par]])
                        dve([wfb[par], buf("rwscols")], [wib[0]],
                            lambda e, pa=par, rw=rwi: e.tensor_scalar(
                                out=s_wi[0][:], in0=s_wf[pa][:], scalar1=s_rwscols[:, rw:rw + 1],
                                scalar2=None, op0=ALU.mult))
                        pool([wib[0]], [dstB],
                             lambda e, d=dstT: e.tensor_scalar(
                                 out=d[:], in0=s_wi[0][:], scalar1=1.0, scalar2=-1.0,
                                 op0=ALU.min, op1=ALU.max))
                        dma_out('sB', wq_d[i], dstT[:], [dstB], wbufs=[dscr])
                    else:
                        dma_in(semld, dstT[:], wq_d[i], [dstB], rbufs=[dscr])

                load_q(0, 'g')
                load_q(0, 'u')
                load_q(1, 'g')
                load_q(1, 'u')
                for i in range(IK):
                    par = i % 4
                    if i + 2 < IK:
                        load_q(i + 2, 'g')
                        load_q(i + 2, 'u')
                    qp = i % 2
                    for which in ('g', 'u'):
                        wqb_ = gqb[par] if which == 'g' else uqb[par]
                        wt = s_gq[par] if which == 'g' else s_uq[par]
                        off = 0 if which == 'g' else 2
                        builders = []
                        for ck in range(CK):
                            st, sp = (ck == 0), (ck == CK - 1)
                            for hh in range(2):
                                builders.append(
                                    lambda e, c=ck, st=st, sp=sp, pa=qp, w=wt, o=off, h2=hh:
                                    e.matmul(pb[4 * pa + o + h2][:, :],
                                             w[:, c * 128:(c + 1) * 128],
                                             s_actb[:, c, h2 * TH:(h2 + 1) * TH],
                                             start=st, stop=sp))
                        mm([wqb_] + ACTB, [PB[4 * qp + off], PB[4 * qp + off + 1]], builders)
                    for hh in range(2):
                        gi = 2 * qp + hh
                        dve([PB[4 * qp + hh], XSBC], [grb[gi]],
                            lambda e, pa=qp, h2=hh, g2=gi: e.scalar_tensor_tensor(
                                out=s_gr[g2][:], in0=pb[4 * pa + h2][:, :], scalar=1.0,
                                in1=s_xsbc[:, h2 * TH:(h2 + 1) * TH], op0=ALU.bypass, op1=ALU.mult))
                        act([grb[gi], buf("wscols")], [gsb[gi]],
                            lambda e, g2=gi: e.activation(out=s_gs[g2][:], in_=s_gr[g2][:],
                                                          func=AF.Silu, scale=s_wscols[:, 0:1]))
                        dve([PB[4 * qp + 2 + hh], gsb[gi]], [hb[qp]],
                            lambda e, pa=qp, h2=hh, g2=gi: e.tensor_tensor(
                                out=s_h[pa][:, h2 * TH:(h2 + 1) * TH], in0=s_gs[g2][:],
                                in1=pb[4 * pa + 2 + h2][:, :], op=ALU.mult))
                    if i == 0:
                        pool([hb[qp]], [hfb[0]],
                             lambda e, pa=qp: e.tensor_scalar(
                                 out=s_hf[0][:, 0:1024], in0=s_h[pa][:], scalar1=1.0,
                                 scalar2=None, op0=ALU.mult))
                        pool([hb[qp], hfb[0]], [hfb[0]],
                             lambda e, pa=qp: e.tensor_scalar(
                                 out=s_hf[0][:, 1024:2048], in0=s_h[pa][:], scalar1=1.0,
                                 scalar2=None, op0=ALU.mult))
                    else:
                        a2, b2 = s_hf[(i + 1) % 2], s_hf[i % 2]
                        dve([hb[qp], hfb[(i + 1) % 2]], [hfb[i % 2]],
                            lambda e, pa=qp, a=a2, b=b2: e.tensor_tensor(
                                out=b[:, 0:1024], in0=a[:, 0:1024], in1=s_h[pa][:], op=ALU.max))
                        dve([hb[qp], hfb[(i + 1) % 2], hfb[i % 2]], [hfb[i % 2]],
                            lambda e, pa=qp, a=a2, b=b2: e.tensor_tensor(
                                out=b[:, 1024:2048], in0=a[:, 1024:2048], in1=s_h[pa][:],
                                op=ALU.min))
                    dma_out('sC' if qp == 0 else 'sD', h_dram[i], s_h[qp][:], [hb[qp]],
                            wbufs=[buf(f"d_h_{i}")])
                hfin, shfin = hfb[(IK - 1) % 2], s_hf[(IK - 1) % 2]

                dance32(shfin, hfin, RMX, s_rowMX)
                dve([RMX], [RA],
                    lambda e: e.tensor_scalar(out=s_rowA[:], in0=s_rowMX[0:1, T:2 * T],
                                              scalar1=-1.0, scalar2=None, op0=ALU.mult))
                dve([RMX, RA], [RE],
                    lambda e: e.tensor_tensor(out=s_rowE[:], in0=s_rowMX[0:1, 0:T],
                                              in1=s_rowA[:], op=ALU.max))
                dve([SROW, buf("wsrow")], [RM],
                    lambda e: e.tensor_scalar(out=s_rowM[:], in0=s_srow[:],
                                              scalar1=s_wsrow[0:1, 1:2], scalar2=None,
                                              op0=ALU.mult))
                dve([RE, RM], [RT],
                    lambda e: e.tensor_tensor(out=s_rowT[:], in0=s_rowE[:], in1=s_rowM[:],
                                              op=ALU.mult))
                act([RT], [RV],
                    lambda e: e.activation(out=s_rowV[:], in_=s_rowT[:], func=AF.Copy,
                                           scale=1.0 / 127.0, bias=EPS_Q))
                dve([RV], [RA],
                    lambda e: e.reciprocal(out=s_rowA[:], in_=s_rowV[:]))
                dve([RA, RM], [RR],
                    lambda e: e.tensor_tensor(out=s_rowR[:], in0=s_rowA[:], in1=s_rowM[:],
                                              op=ALU.mult))
                dve([RV, buf("wsrow")], [RT],
                    lambda e: e.tensor_scalar(out=s_rowT[:], in0=s_rowV[:],
                                              scalar1=s_wsrow[0:1, 2:3], scalar2=0.5,
                                              op0=ALU.mult, op1=ALU.mult))
                bcast_pair(s_rowR, 4, 5, RR, s_bcA, BCA)
                bcast_pair(s_rowT, 6, 7, RT, s_bcB, BCB)

            barrier()
            with ExitStack() as sD:
                E2 = sD.enter_context
                s_hq = E2(nc.sbuf_tensor(f"f{li}d_hq", [128, IK, TH], BF16))
                s_hin = [E2(nc.sbuf_tensor(f"f{li}d_hin{j}", [128, TH], F32)) for j in range(4)]
                s_hi32 = [E2(nc.sbuf_tensor(f"f{li}d_hi32{j}", [128, TH], I32)) for j in range(2)]
                s_wdf = [E2(nc.sbuf_tensor(f"f{li}d_wdf{j}", [128, 1024], F32)) for j in range(4)]
                s_wdq = [E2(nc.sbuf_tensor(f"f{li}d_wdq{j}", [128, 1024], BF16)) for j in range(4)]
                s_wdi = E2(nc.sbuf_tensor(f"f{li}d_wdi", [128, 1024], I32))
                s_xc = [E2(nc.sbuf_tensor(f"f{li}d_xc{j}", [128, TH], F32)) for j in range(2)]
                s_y1 = [E2(nc.sbuf_tensor(f"f{li}d_y1{j}", [128, TH], F32)) for j in range(2)]
                s_y2 = [E2(nc.sbuf_tensor(f"f{li}d_y2{j}", [128, TH], F32)) for j in range(2)]
                hqb = [buf(f"f{li}d_hq{i}") for i in range(IK)]
                hinb = [buf(f"f{li}d_hin{j}") for j in range(4)]
                hi32b = [buf(f"f{li}d_hi32{j}") for j in range(2)]
                wdfb = [buf(f"f{li}d_wdf{j}") for j in range(4)]
                wdqb = [buf(f"f{li}d_wdq{j}") for j in range(4)]
                wdib = buf(f"f{li}d_wdi")
                xcb_ = [buf(f"f{li}d_xc{j}") for j in range(2)]
                y1b = [buf(f"f{li}d_y1{j}") for j in range(2)]
                y2b = [buf(f"f{li}d_y2{j}") for j in range(2)]

                for hh in range(2):
                    for i in range(IK):
                        par = i % 4
                        qp = i % 2
                        dma_in(('lA', 'lB', 'lC', 'lD')[par], s_hin[par][:],
                               h_dram[i][:, hh * TH:(hh + 1) * TH], [hinb[par]],
                               rbufs=[buf(f"d_h_{i}")])
                        dve([hinb[par], BCA], [hi32b[qp]],
                            lambda e, pa=par, q2=qp, h2=hh: e.tensor_tensor(
                                out=s_hi32[q2][:], in0=s_hin[pa][:],
                                in1=s_bcA[:, h2 * TH:(h2 + 1) * TH], op=ALU.mult))
                        act([hi32b[qp]], [hqb[i]],
                            lambda e, q2=qp, i2=i: e.activation(out=s_hq[:, i2, :],
                                                                in_=s_hi32[q2][:], func=AF.Copy))
                        if i % 8 == 7:
                            ig0 = i // 8
                            par2 = ig0 % 4
                            dscr0 = buf(f"d_wdq_0_{ig0}")
                            if first and hh == 0:
                                dma_in(('lE', 'lF', 'lG', 'lH')[par2], s_wdf[par2][:],
                                       wd_r[0, ig0].rearrange("p ib q -> p (ib q)"), [wdfb[par2]])
                                dve([wdfb[par2], buf("rwscols")], [wdib],
                                    lambda e, pa=par2: e.tensor_scalar(
                                        out=s_wdi[:], in0=s_wdf[pa][:],
                                        scalar1=s_rwscols[:, 2:3], scalar2=None, op0=ALU.mult))
                                pool([wdib], [wdqb[par2]],
                                     lambda e, pa=par2: e.tensor_scalar(
                                         out=s_wdq[pa][:], in0=s_wdi[:], scalar1=1.0,
                                         scalar2=-1.0, op0=ALU.min, op1=ALU.max))
                                dma_out('sB', wdq_dram[0, ig0], s_wdq[par2][:], [wdqb[par2]],
                                        wbufs=[dscr0])
                            else:
                                dma_in(('lE', 'lF', 'lG', 'lH')[par2], s_wdq[par2][:],
                                       wdq_dram[0, ig0], [wdqb[par2]], rbufs=[dscr0])
                            builders = []
                            for ib in range(8):
                                i3 = ig0 * 8 + ib
                                st, sp = (i3 == 0), (i3 == IK - 1)
                                builders.append(
                                    lambda e, pa=par2, ib2=ib, i4=i3, st=st, sp=sp:
                                    e.matmul(pb[0][:, :],
                                             s_wdq[pa][:, ib2 * 128:(ib2 + 1) * 128],
                                             s_hq[:, i4, :], start=st, stop=sp))
                            mm([wdqb[par2]] + [hqb[ig0 * 8 + ib] for ib in range(8)],
                               [PB[0]] if ig0 in (0, 7) else [], builders)
                    for d in [0]:
                        par = 0
                        if first:
                            xsrc = x_ext[d * 128:(d + 1) * 128, hh * TH:(hh + 1) * TH]
                            xdep = []
                        else:
                            xsrc = res2_dram[d][:, hh * TH:(hh + 1) * TH]
                            xdep = [buf(f"d_res2_{d}")]
                        dma_in('lI', s_xc[0][:], xsrc, [xcb_[0]], rbufs=xdep)
                        dve([PB[0], BCB], [y1b[0]],
                            lambda e, h2=hh: e.scalar_tensor_tensor(
                                out=s_y1[0][:], in0=pb[0][:, :], scalar=1.0,
                                in1=s_bcB[:, h2 * TH:(h2 + 1) * TH], op0=ALU.mult, op1=ALU.mult))
                        dve([y1b[0], xcb_[0]], [y2b[0]],
                            lambda e: e.tensor_tensor(
                                out=s_y2[0][:], in0=s_y1[0][:], in1=s_xc[0][:], op=ALU.add))
                        if first:
                            dst = res1_dram[d][:, hh * TH:(hh + 1) * TH]
                            wscr = [buf(f"d_res1_{d}_{hh}")]
                        else:
                            dst = y_ext[d * 128:(d + 1) * 128, hh * TH:(hh + 1) * TH]
                            wscr = []
                        dma_out('sA', dst, s_y2[0][:], [y2b[0]], wbufs=wscr)
                    for d in range(1, CK):
                        pbank = PB[d % 2]
                        for ig in range(8):
                            par = (d * 8 + ig) % 4
                            dscr = buf(f"d_wdq_{d}_{ig}")
                            if first and hh == 0:
                                dma_in(('lE', 'lF', 'lG', 'lH')[par], s_wdf[par][:],
                                       wd_r[d, ig].rearrange("p ib q -> p (ib q)"), [wdfb[par]])
                                dve([wdfb[par], buf("rwscols")], [wdib],
                                    lambda e, pa=par: e.tensor_scalar(
                                        out=s_wdi[:], in0=s_wdf[pa][:],
                                        scalar1=s_rwscols[:, 2:3], scalar2=None, op0=ALU.mult))
                                pool([wdib], [wdqb[par]],
                                     lambda e, pa=par: e.tensor_scalar(
                                         out=s_wdq[pa][:], in0=s_wdi[:], scalar1=1.0,
                                         scalar2=-1.0, op0=ALU.min, op1=ALU.max))
                                dma_out('sB', wdq_dram[d, ig], s_wdq[par][:], [wdqb[par]],
                                        wbufs=[dscr])
                            else:
                                dma_in(('lE', 'lF', 'lG', 'lH')[par], s_wdq[par][:],
                                       wdq_dram[d, ig], [wdqb[par]], rbufs=[dscr])
                            builders = []
                            for ib in range(8):
                                i2 = ig * 8 + ib
                                st, sp = (i2 == 0), (i2 == IK - 1)
                                builders.append(
                                    lambda e, pa=par, ib2=ib, i3=i2, st=st, sp=sp, d2=d:
                                    e.matmul(pb[d2 % 2][:, :],
                                             s_wdq[pa][:, ib2 * 128:(ib2 + 1) * 128],
                                             s_hq[:, i3, :], start=st, stop=sp))
                            mm([wdqb[par]] + [hqb[ig * 8 + ib] for ib in range(8)],
                               [pbank] if ig in (0, 7) else [], builders)
                        par = d % 2
                        if first:
                            xsrc = x_ext[d * 128:(d + 1) * 128, hh * TH:(hh + 1) * TH]
                            xdep = []
                        else:
                            xsrc = res2_dram[d][:, hh * TH:(hh + 1) * TH]
                            xdep = [buf(f"d_res2_{d}")]
                        dma_in('lI' if par == 0 else 'lJ', s_xc[par][:], xsrc, [xcb_[par]],
                               rbufs=xdep)
                        dve([pbank, BCB], [y1b[par]],
                            lambda e, pa=par, d2=d, h2=hh: e.scalar_tensor_tensor(
                                out=s_y1[pa][:], in0=pb[d2 % 2][:, :], scalar=1.0,
                                in1=s_bcB[:, h2 * TH:(h2 + 1) * TH], op0=ALU.mult, op1=ALU.mult))
                        dve([y1b[par], xcb_[par]], [y2b[par]],
                            lambda e, pa=par: e.tensor_tensor(
                                out=s_y2[pa][:], in0=s_y1[pa][:], in1=s_xc[pa][:], op=ALU.add))
                        if first:
                            dst = res1_dram[d][:, hh * TH:(hh + 1) * TH]
                            wscr = [buf(f"d_res1_{d}_{hh}")]
                        else:
                            dst = y_ext[d * 128:(d + 1) * 128, hh * TH:(hh + 1) * TH]
                            wscr = []
                        dma_out('sA', dst, s_y2[par][:], [y2b[par]], wbufs=wscr)
            ffn_stack.close()

        def emit_attn():
            barrier()
            with ExitStack() as sA:
                E2 = sA.enter_context
                s_x = E2(nc.sbuf_tensor("a0_x", [128, CK, T], F32))
                s_cb = [E2(nc.sbuf_tensor(f"a0_cb{j}", [128, T], BF16)) for j in range(2)]
                s_sq = [E2(nc.sbuf_tensor(f"a0_sq{j}", [128, T], BF16)) for j in range(2)]
                s_xp_ = [E2(nc.sbuf_tensor(f"a0_xp{j}", [128, T + 3], F32)) for j in range(2)]
                s_cv = [E2(nc.sbuf_tensor(f"a0_cv{j}", [128, T], F32)) for j in range(4)]
                s_cva = E2(nc.sbuf_tensor("a0_cva", [128, T], F32))
                s_cvb = E2(nc.sbuf_tensor("a0_cvb", [128, T], F32))
                s_xcb = E2(nc.sbuf_tensor("a0_xcb", [128, T], BF16))
                xb = [buf(f"a0_x{ck}") for ck in range(CK)]
                cbb = [buf(f"a0_cb{j}") for j in range(2)]
                sqb = [buf(f"a0_sq{j}") for j in range(2)]
                xpb_ = [buf(f"a0_xp{j}") for j in range(2)]
                cvb = [buf(f"a0_cv{j}") for j in range(4)]
                cvab = buf("a0_cva")
                cvbb = buf("a0_cvb")
                xcbb = buf("a0_xcb")

                ld_evs = []
                for ck in range(CK):
                    ld_evs.append(dma_in('lA' if ck % 2 == 0 else 'lB', s_x[:, ck, :],
                                         res1_dram[ck], [xb[ck]],
                                         rbufs=[buf(f"d_res1_{ck}_0"), buf(f"d_res1_{ck}_1")]))
                x_all = (ld_evs[-1], ld_evs[-2])
                for ck in range(CK):
                    par = ck % 2
                    act([xb[ck]], [cbb[par]],
                        lambda e, c=ck, pa=par: e.activation(out=s_cb[pa][:], in_=s_x[:, c, :],
                                                             func=AF.Copy))
                    act([cbb[par]], [sqb[par]],
                        lambda e, pa=par: e.activation(out=s_sq[pa][:], in_=s_cb[pa][:],
                                                       func=AF.Square))
                    st, sp = (ck == 0), (ck == CK - 1)
                    mm([cbb[par], CONSTS], [PB[0], PB[1]],
                       [(lambda e, pa=par, st=st, sp=sp:
                         e.matmul(pb[0][0:1, :], s_ones_colb[:], s_cb[pa][:, 0:TH], start=st, stop=sp)),
                        (lambda e, pa=par, st=st, sp=sp:
                         e.matmul(pb[1][0:1, :], s_ones_colb[:], s_cb[pa][:, TH:T], start=st, stop=sp))])
                    mm([sqb[par], CONSTS], [PB[2], PB[3]],
                       [(lambda e, pa=par, st=st, sp=sp:
                         e.matmul(pb[2][0:1, :], s_ones_colb[:], s_sq[pa][:, 0:TH], start=st, stop=sp)),
                        (lambda e, pa=par, st=st, sp=sp:
                         e.matmul(pb[3][0:1, :], s_ones_colb[:], s_sq[pa][:, TH:T], start=st, stop=sp))])
                ln_stat_rows(RM, RV, RR)
                dve([RM, RR], [RV],
                    lambda e: e.scalar_tensor_tensor(out=s_rowV[:], in0=s_rowM[:], scalar=-1.0,
                                                     in1=s_rowR[:], op0=ALU.mult, op1=ALU.mult))
                bcast_pair(s_rowR, 4, 5, RR, s_bcA, BCA)
                bcast_pair(s_rowV, 6, 7, RV, s_bcB, BCB)
                h6b = buf("head6")
                tstg = buf("tail48")
                for ck in range(CK):
                    par = ck % 2
                    s_xp = s_xp_[par]
                    xpb = xpb_[par]
                    dve([], [xpb], lambda e, sx=s_xp: e.memset(sx[:, 0:3], 0.0))
                    dve([xb[ck], BCA, xpb], [cvb[3]],
                        lambda e, c=ck: e.tensor_tensor(out=s_cv[3][:], in0=s_x[:, c, :],
                                                        in1=s_bcA[:], op=ALU.mult))
                    dve([cvb[3], BCB], [xpb],
                        lambda e, sx=s_xp: e.tensor_tensor(out=sx[:, 3:T + 3], in0=s_cv[3][:],
                                                           in1=s_bcB[:], op=ALU.add))
                    for j in range(4):
                        dve([xpb, CONSTS], [cvb[j]],
                            lambda e, c=ck, j2=j, sx=s_xp: e.tensor_scalar(
                                out=s_cv[j2][:], in0=sx[:, j2:T + j2],
                                scalar1=s_convw[:, 4 * c + j2:4 * c + j2 + 1],
                                scalar2=None, op0=ALU.mult))
                    pool([cvb[0], cvb[1]], [cvab],
                         lambda e: e.tensor_tensor(out=s_cva[:], in0=s_cv[0][:],
                                                   in1=s_cv[1][:], op=ALU.add))
                    pool([cvb[2], cvb[3]], [cvbb],
                         lambda e: e.tensor_tensor(out=s_cvb[:], in0=s_cv[2][:],
                                                   in1=s_cv[3][:], op=ALU.add))
                    pool([cvab, cvbb], [cvb[0]],
                         lambda e: e.tensor_tensor(out=s_cv[0][:], in0=s_cva[:],
                                                   in1=s_cvb[:], op=ALU.add))
                    act([cvb[0], CONSTS], [xcbb],
                        lambda e, c=ck: e.activation(out=s_xcb[:], in_=s_cv[0][:], func=AF.Silu,
                                                     bias=s_convb[:, c:c + 1]))
                    dma_out('sA', xcb_dram[ck], s_xcb[:], [xcbb], wbufs=[buf(f"d_xcb_{ck}")])
                    act([xpb], [ACTB[ck]],
                        lambda e, c=ck, sx=s_xp: e.activation(out=s_actb[:, c, :],
                                                              in_=sx[:, 3:T + 3], func=AF.Copy))
                    pool([xpb], [h6b],
                         lambda e, c=ck, sx=s_xp: e.tensor_scalar(
                             out=s_head6[:, 6 * c:6 * c + 6], in0=sx[:, 0:6], scalar1=1.0,
                             scalar2=None, op0=ALU.mult))
                    pool([xpb], [tstg],
                         lambda e, c=ck, sx=s_xp: e.tensor_scalar(
                             out=s_stage48[:, 3 * c:3 * c + 3], in0=sx[:, T:T + 3], scalar1=1.0,
                             scalar2=None, op0=ALU.mult))

            barrier()
            with ExitStack() as sB:
                E2 = sB.enter_context
                s_big1 = E2(nc.sbuf_tensor("a1_big1", [128, CK, T], F32))
                s_tmp = [E2(nc.sbuf_tensor(f"a1_tmp{j}", [128, T], F32)) for j in range(2)]
                big1 = [buf(f"a1_big1_{ck}") for ck in range(CK)]
                big2 = [buf(f"a1_big2_{ck}") for ck in range(CK)]
                tmpb = [buf(f"a1_tmp{j}") for j in range(2)]
                sub1 = ExitStack()
                E3 = sub1.enter_context
                s_wsl = [E3(nc.sbuf_tensor(f"a1_wsl{j}", [128, C], F32)) for j in range(3)]
                s_wslb = [E3(nc.sbuf_tensor(f"a1_wslb{j}", [128, C], BF16)) for j in range(3)]
                s_sqp = [E3(nc.sbuf_tensor(f"a1_sqp{j}", [128, TH], BF16)) for j in range(2)]
                s_rb = [E3(nc.sbuf_tensor(f"a1_rb{j}", [128, TH], F32)) for j in range(2)]
                s_nX = [E3(nc.sbuf_tensor(f"a1_nX{j}", [1, TH], F32)) for j in range(4)]
                s_nY = [E3(nc.sbuf_tensor(f"a1_nY{j}", [1, TH], F32)) for j in range(2)]
                s_nS = [E3(nc.sbuf_tensor(f"a1_nS{j}", [1, TH], F32)) for j in range(2)]
                s_kraw = [E3(nc.sbuf_tensor(f"a1_kraw{j}", [128, T], F32)) for j in range(2)]
                s_gt = s_kraw[1]
                wslb = [buf(f"a1_wsl{j}") for j in range(3)]
                wslbb = [buf(f"a1_wslb{j}") for j in range(3)]
                sqpb = [buf(f"a1_sqp{j}") for j in range(2)]
                rbb = [buf(f"a1_rb{j}") for j in range(2)]
                nXb = [buf(f"a1_nX{j}") for j in range(4)]
                nYb = [buf(f"a1_nY{j}") for j in range(2)]
                krawb = [buf(f"a1_kraw{j}") for j in range(2)]
                nSb = [buf(f"a1_nS{j}") for j in range(2)]
                gtb = buf("a1_kraw1")

                SW = {'wsl': s_wsl, 'wslb': s_wslb, 'sqp': s_sqp, 'rb': s_rb,
                      'nX': s_nX, 'nY': s_nY, 'nS': s_nS, 'kraw': s_kraw}

                def stream_w(w_ap, d, par):
                    nsw = len(SW['wsl'])
                    sp_ = d % nsw
                    wt, wbt = SW['wsl'][sp_], SW['wslb'][sp_]
                    dma_in(('lA', 'lB', 'lC', 'lD')[sp_ % 3], wt[:],
                           w_ap[d].rearrange("p ck q -> p (ck q)"), [wslb[sp_]])
                    dve([wslb[sp_]], [wslbb[sp_]],
                        lambda e, a=wbt, b=wt: e.tensor_copy(a[:], b[:]))
                    return sp_

                def proj_mms(d, par, rhs_bufs, rhs_tile, extra_evs=()):
                    sp_ = d % len(SW['wsl'])
                    wbt = SW['wslb'][sp_]
                    builders = []
                    for ck in range(CK):
                        st, sp = (ck == 0), (ck == CK - 1)
                        for hh in range(2):
                            builders.append(
                                lambda e, c=ck, st=st, sp=sp, pa=par, h2=hh, w=wbt:
                                e.matmul(pb[2 * pa + h2][:, :],
                                         w[:, c * 128:(c + 1) * 128],
                                         rhs_tile(c, h2), start=st, stop=sp))
                    mm([wslbb[sp_]] + rhs_bufs, [PB[2 * par], PB[2 * par + 1]], builders,
                       extra_evs=extra_evs)

                xlb_tile = lambda c, hh: s_actb[:, c, hh * TH:(hh + 1) * TH]

                def evac_raw(d, par):
                    kp = d % 2
                    krt = SW['kraw'][kp]
                    for hh in range(2):
                        pk = 2 * par + hh
                        act([PB[pk]], [krawb[kp]],
                            lambda e, k=pk, a=krt, h2=hh: e.activation(
                                out=a[:, h2 * TH:(h2 + 1) * TH], in_=pb[k][:, :], func=AF.Copy))

                def dance_a(d):
                    kp = d % 2
                    krt, sqt = SW['kraw'][kp], SW['sqp'][kp]
                    for hh in range(2):
                        np_ = 4 + kp
                        act([krawb[kp]], [sqpb[kp]],
                            lambda e, a=sqt, b=krt, h2=hh: e.activation(
                                out=a[:], in_=b[:, h2 * TH:(h2 + 1) * TH], func=AF.Square))
                        mm([sqpb[kp], CONSTS], [PB[np_]],
                           [lambda e, k=np_, a=sqt: e.matmul(pb[k][0:1, :], s_ones_colb[:],
                                                             a[:], start=True, stop=True)])
                        act([PB[np_]], [nXb[2 * kp + hh]],
                            lambda e, k=np_, a=SW['nX'][2 * kp + hh]: e.activation(
                                out=a[:], in_=pb[k][0:1, :], func=AF.Sqrt))

                def dance_b(d, dstT, dstbuf):
                    kp = d % 2
                    krt, rbt = SW['kraw'][kp], SW['rb'][kp]
                    for hh in range(2):
                        ni = 2 * kp + hh
                        nxt, nyt = SW['nX'][ni], SW['nY'][kp]
                        bb = 6 + kp
                        dve([nXb[ni]], [nYb[kp]],
                            lambda e, a=nyt, b=nxt: e.tensor_scalar(out=a[:], in0=b[:],
                                                                    scalar1=EPS_NRM, scalar2=None,
                                                                    op0=ALU.max))
                        dve([nYb[kp]], [nXb[ni]],
                            lambda e, a=nxt, b=nyt: e.reciprocal(out=a[:], in_=b[:]))
                        mm([nXb[ni], CONSTS], [PB[bb]],
                           [lambda e, k=bb, a=nxt: e.matmul(pb[k][:, :], s_ones_row[:],
                                                            a[:], start=True, stop=True)])
                        act([PB[bb]], [rbb[kp]],
                            lambda e, k=bb, a=rbt: e.activation(out=a[:],
                                                                in_=pb[k][:, :], func=AF.Copy))
                        dve([krawb[kp], rbb[kp]], [dstbuf],
                            lambda e, a=rbt, b=krt, d2=d, h2=hh: e.tensor_tensor(
                                out=dstT[:, d2, h2 * TH:(h2 + 1) * TH],
                                in0=b[:, h2 * TH:(h2 + 1) * TH], in1=a[:], op=ALU.mult))

                def proj_norm_loop(w_ap, dstT, dstbufs):
                    for d in range(CK):
                        par = d % 2
                        if d >= 2:
                            dance_b(d - 2, dstT, dstbufs[d - 2])
                        stream_w(w_ap, d, par)
                        proj_mms(d, par, ACTB, xlb_tile)
                        if d >= 1:
                            dance_a(d - 1)
                        evac_raw(d, par)
                    dance_a(CK - 1)
                    dance_b(CK - 2, dstT, dstbufs[CK - 2])
                    dance_b(CK - 1, dstT, dstbufs[CK - 1])

                proj_norm_loop(wk_r, s_big1, big1)
                for d in range(CK):
                    par = d % 2
                    stream_w(wv_r, d, par)
                    proj_mms(d, par, ACTB, xlb_tile)
                    for hh in range(2):
                        pk = 2 * par + hh
                        dve([PB[pk], big1[d]], [tmpb[par]],
                            lambda e, k=pk, pa=par, d2=d, h2=hh: e.tensor_tensor(
                                out=s_tmp[pa][:, h2 * TH:(h2 + 1) * TH], in0=pb[k][:, :],
                                in1=s_big1[:, d2, h2 * TH:(h2 + 1) * TH], op=ALU.mult))
                    pool([tmpb[par]], [big1[d]],
                         lambda e, pa=par, d2=d: e.tensor_copy(s_big1[:, d2, :], s_tmp[pa][:]))
                stg = buf("stage16")
                for hd in range(H):
                    par = hd % 2
                    dve([big1[hd], CONSTS], [gtb],
                        lambda e, h2=hd: e.tensor_scalar(out=s_gt[:], in0=s_big1[:, h2, :],
                                                         scalar1=0.0,
                                                         scalar2=s_gcols[:, h2:h2 + 1],
                                                         op0=ALU.mult, op1=ALU.add))
                    dve([gtb, big1[hd]], [tmpb[par]],
                        lambda e, h2=hd, pa=par: e.tensor_tensor_scan(
                            out=s_tmp[pa][:], data0=s_gt[:], data1=s_big1[:, h2, :],
                            initial=0.0, op0=ALU.mult, op1=ALU.add))
                    pool([tmpb[par]], [stg],
                         lambda e, pa=par, h2=hd: e.tensor_copy(
                             s_stage16[:, h2:h2 + 1], s_tmp[pa][:, T - 1:T]))
                    dma_out('sC' if par == 0 else 'sD', s2_dram[hd], s_tmp[par][:],
                            [tmpb[par]], wbufs=[big2[hd]])
                sub1.close()
                barrier()
                sub2 = ExitStack()
                E3 = sub2.enter_context
                mm([stg, CONSTS], [PB[0]],
                   [lambda e: e.transpose(pb[0][0:16, 0:128], s_stage16[:], s_ident[:])])
                ev16 = buf("evac16")
                act([PB[0]], [ev16],
                    lambda e: e.activation(out=s_evac16[:], in_=pb[0][0:16, 0:128], func=AF.Copy))
                ccib = buf("d_ccin2")
                dma_out('sA', cc_in.rearrange("a (r c) -> (a r) c", c=128)[0:16, :],
                        s_evac16[:], [ev16], wbufs=[ccib])
                tstg = buf("tail48")
                mm([tstg, CONSTS], [PB[1]],
                   [lambda e: e.transpose(pb[1][0:48, 0:128], s_stage48[:], s_ident[:])])
                ev48 = buf("evac48")
                act([PB[1]], [ev48],
                    lambda e: e.activation(out=s_evac48[:], in_=pb[1][0:48, 0:128], func=AF.Copy))
                dma_out('sA', cc_in.rearrange("a (r c) -> (a r) c", c=128)[16:64, :],
                        s_evac48[:], [ev48], wbufs=[ccib])
                ccog = buf("d_ccout2")
                p.op('gpsimd', 'cc', [ccib], [ccog],
                     lambda e: e.collective_compute(
                         "AllGather", ALU.bypass, replica_groups=[list(range(NC))],
                         ins=[cc_in[:]], outs=[cc_out[:]]))
                s_gat = [E3(nc.sbuf_tensor(f"a1_gat{j}", [NC, TH], F32)) for j in range(2)]
                s_piece = E3(nc.sbuf_tensor("a1_piece", [1, TH], F32))
                s_u16 = E3(nc.sbuf_tensor("a1_u16", [16, 128], F32))
                s_u48 = E3(nc.sbuf_tensor("a1_u48", [48, 128], F32))
                gatb = [buf(f"a1_gat{j}") for j in range(2)]
                pieceb = buf("a1_piece")
                u16b = buf("a1_u16")
                u48b = buf("a1_u48")
                bncb = buf("bnc")
                for j in range(16):
                    gpar = j % 2
                    dma_in('lE' if gpar == 0 else 'lF', s_gat[gpar][:],
                           cc_out[:, j * TH:(j + 1) * TH], [gatb[gpar]], rbufs=[ccog])
                    mm([gatb[gpar], CONSTS], [PB[2]],
                       [lambda e, gp=gpar: e.matmul(pb[2][0:1, :], s_sel[:],
                                                    s_gat[gp][:], start=True, stop=True)])
                    act([PB[2]], [pieceb],
                        lambda e: e.activation(out=s_piece[:], in_=pb[2][0:1, :], func=AF.Copy))
                    tdma([pieceb], [bncb],
                         lambda e, j2=j: e.dma_start(out=bnc[:, j2 * TH:(j2 + 1) * TH],
                                                     in_=s_piece[:]))
                dma_in('lE', s_u16[:], bnc.rearrange("a (r c) -> (a r) c", c=128)[0:16, :],
                       [u16b], rbufs=[bncb])
                dma_in('lF', s_u48[:], bnc.rearrange("a (r c) -> (a r) c", c=128)[16:64, :],
                       [u48b], rbufs=[bncb])
                mm([u16b, CONSTS], [PB[3]],
                   [lambda e: e.transpose(pb[3][:, 0:16], s_u16[:], s_ident[0:16, 0:16])])
                sinb = buf("sin")
                act([PB[3]], [sinb],
                    lambda e: e.activation(out=s_sin[:], in_=pb[3][:, 0:16], func=AF.Copy))
                mm([u48b, CONSTS], [PB[4]],
                   [lambda e: e.transpose(pb[4][:, 0:48], s_u48[:], s_ident[0:48, 0:48])])
                tailb = buf("tailc")
                act([PB[4]], [tailb],
                    lambda e: e.activation(out=s_tailc[:], in_=pb[4][:, 0:48], func=AF.Copy))
                s_fx = [E3(nc.sbuf_tensor(f"a1_fx{j}", [128, 3], F32)) for j in range(2)]
                s_fxb = E3(nc.sbuf_tensor("a1_fxb", [128, 3], BF16))
                fxb = [buf(f"a1_fx{j}") for j in range(2)]
                fxbb = buf("a1_fxb")
                h6b = buf("head6")
                for ck in range(CK):
                    pool([tailb], [h6b],
                         lambda e, c=ck: e.tensor_copy(s_head6[:, 6 * c:6 * c + 3],
                                                       s_tailc[:, 3 * c:3 * c + 3]))
                    dve([h6b, CONSTS], [fxb[0]],
                        lambda e, c=ck: e.tensor_scalar(
                            out=s_fx[0][:], in0=s_head6[:, 6 * c + 3:6 * c + 6],
                            scalar1=s_convw[:, 4 * c + 3:4 * c + 4], scalar2=None, op0=ALU.mult))
                    for j in range(3):
                        a2, b2 = s_fx[j % 2], s_fx[(j + 1) % 2]
                        dve([h6b, fxb[j % 2], CONSTS], [fxb[(j + 1) % 2]],
                            lambda e, c=ck, j2=j, a=a2, b=b2: e.scalar_tensor_tensor(
                                out=b[:], in0=s_head6[:, 6 * c + 2 - j2:6 * c + 5 - j2],
                                scalar=s_convw[:, 4 * c + 2 - j2:4 * c + 3 - j2],
                                in1=a[:], op0=ALU.mult, op1=ALU.add))
                    act([fxb[1], CONSTS], [fxbb],
                        lambda e, c=ck: e.activation(out=s_fxb[:], in_=s_fx[1][:], func=AF.Silu,
                                                     bias=s_convb[:, c:c + 1]))
                    dma_out('sB', xcb_dram[ck][:, 0:3], s_fxb[:], [fxbb],
                            wbufs=[buf(f"d_xcb_{ck}")])
                sub2.close()
                barrier()
                sub3 = ExitStack()
                E3 = sub3.enter_context
                s_wsl = [E3(nc.sbuf_tensor(f"a3_wsl{j}", [128, C], F32)) for j in range(3)]
                s_wslb = [E3(nc.sbuf_tensor(f"a3_wslb{j}", [128, C], BF16)) for j in range(3)]
                s_sqp = [E3(nc.sbuf_tensor(f"a3_sqp{j}", [128, TH], BF16)) for j in range(2)]
                s_rb = [E3(nc.sbuf_tensor(f"a3_rb{j}", [128, TH], F32)) for j in range(2)]
                s_nX = [E3(nc.sbuf_tensor(f"a3_nX{j}", [1, TH], F32)) for j in range(4)]
                s_nY = [E3(nc.sbuf_tensor(f"a3_nY{j}", [1, TH], F32)) for j in range(2)]
                s_nS = [E3(nc.sbuf_tensor(f"a3_nS{j}", [1, TH], F32)) for j in range(2)]
                s_kraw = [E3(nc.sbuf_tensor(f"a3_kraw{j}", [128, T], F32)) for j in range(2)]
                (SW['wsl'], SW['wslb'], SW['sqp'], SW['rb'], SW['nX'], SW['nY'],
                 SW['kraw'], SW['nS']) = (s_wsl, s_wslb, s_sqp, s_rb, s_nX, s_nY, s_kraw, s_nS)
                proj_norm_loop(wq_r, s_big1, big1)
                sub3.close()
                barrier()
                sub3b = ExitStack()
                E3 = sub3b.enter_context
                s_wsl = [E3(nc.sbuf_tensor(f"a3b_wsl{j}", [128, C], F32)) for j in range(2)]
                s_wslb = [E3(nc.sbuf_tensor(f"a3b_wslb{j}", [128, C], BF16)) for j in range(2)]
                SW['wsl'], SW['wslb'] = s_wsl, s_wslb
                s_xcb2 = E3(nc.sbuf_tensor("a1_xcb2", [128, CK, T], BF16))
                xcb2b = buf("a1_xcb2")
                xevs = []
                for ck in range(CK):
                    xevs.append(dma_in('lC' if ck % 2 == 0 else 'lD', s_xcb2[:, ck, :],
                                       xcb_dram[ck], [xcb2b], rbufs=[buf(f"d_xcb_{ck}")]))
                xcb_all = (xevs[-1], xevs[-2])
                s_gtmp = [E3(nc.sbuf_tensor(f"a1_gtmp{j}", [128, TH], F32)) for j in range(2)]
                gtmpb = [buf(f"a1_gtmp{j}") for j in range(2)]
                xcb_tile = lambda c2, hh: s_xcb2[:, c2, hh * TH:(hh + 1) * TH]
                for d in range(CK):
                    par = d % 2
                    stream_w(wgt_r, d, par)
                    proj_mms(d, par, [xcb2b], xcb_tile, extra_evs=xcb_all)
                    for hh in range(2):
                        pk = 2 * par + hh
                        act([PB[pk], CONSTS], [gtmpb[par]],
                            lambda e, k=pk, pa=par, d2=d: e.activation(
                                out=s_gtmp[pa][:], in_=pb[k][:, :], func=AF.Sigmoid,
                                bias=s_gateb[:, d2:d2 + 1]))
                        dve([gtmpb[par], big1[d]], [tmpb[par]],
                            lambda e, pa=par, d2=d, h2=hh: e.tensor_tensor(
                                out=s_tmp[pa][:, h2 * TH:(h2 + 1) * TH], in0=s_gtmp[pa][:],
                                in1=s_big1[:, d2, h2 * TH:(h2 + 1) * TH], op=ALU.mult))
                    pool([tmpb[par]], [big1[d]],
                         lambda e, pa=par, d2=d: e.tensor_copy(s_big1[:, d2, :], s_tmp[pa][:]))
                sub3b.close()
                barrier()
                sub4 = ExitStack()
                E3 = sub4.enter_context
                s_gp = E3(nc.sbuf_tensor("a1_gp", [128, T], F32))
                s_gp2 = E3(nc.sbuf_tensor("a1_gp2", [128, T], F32))
                s_sch = [E3(nc.sbuf_tensor(f"a1_sch{j}", [128, T], F32)) for j in range(2)]
                gpb = buf("a1_gp")
                gp2b = buf("a1_gp2")
                schb = [buf(f"a1_sch{j}") for j in range(2)]
                for hd in range(H):
                    par = hd % 2
                    dma_in('lA' if par == 0 else 'lB', s_sch[par][:], s2_dram[hd],
                           [schb[par]], rbufs=[big2[hd]])
                    act([CONSTS], [gpb],
                        lambda e, h2=hd: e.activation(out=s_gp[:], in_=s_iota[:], func=AF.Exp,
                                                      scale=s_lnm[:, h2:h2 + 1]))
                    dve([gpb, sinb], [gp2b],
                        lambda e, h2=hd: e.tensor_scalar(out=s_gp2[:], in0=s_gp[:],
                                                         scalar1=s_sin[:, h2:h2 + 1],
                                                         scalar2=None, op0=ALU.mult))
                    dve([gp2b, schb[par]], [tmpb[par]],
                        lambda e, pa=par: e.tensor_tensor(
                            out=s_tmp[pa][:], in0=s_sch[pa][:], in1=s_gp2[:], op=ALU.add))
                    dve([tmpb[par], big1[hd]], [ACTB[hd]],
                        lambda e, pa=par, h2=hd: e.tensor_tensor(
                            out=s_actb[:, h2, :], in0=s_big1[:, h2, :], in1=s_tmp[pa][:],
                            op=ALU.mult))
                sub4.close()

            barrier()
            with ExitStack() as sD:
                E2 = sD.enter_context
                s_wsl = [E2(nc.sbuf_tensor(f"ad_wsl{j}", [128, C], F32)) for j in range(2)]
                s_wslb = [E2(nc.sbuf_tensor(f"ad_wslb{j}", [128, C], BF16)) for j in range(2)]
                s_y = E2(nc.sbuf_tensor("ad_y", [128, CK, T], F32))
                s_cb = [E2(nc.sbuf_tensor(f"ad_cb{j}", [128, T], BF16)) for j in range(2)]
                s_sq = [E2(nc.sbuf_tensor(f"ad_sq{j}", [128, T], BF16)) for j in range(2)]
                s_r1 = [E2(nc.sbuf_tensor(f"ad_r1{j}", [128, T], F32)) for j in range(2)]
                s_res = [E2(nc.sbuf_tensor(f"ad_res{j}", [128, T], F32)) for j in range(2)]
                s_z = [E2(nc.sbuf_tensor(f"ad_z{j}", [128, T], F32)) for j in range(2)]
                wslb = [buf(f"ad_wsl{j}") for j in range(2)]
                wslbb = [buf(f"ad_wslb{j}") for j in range(2)]
                yb = [buf(f"ad_y{ck}") for ck in range(CK)]
                cbb = [buf(f"ad_cb{j}") for j in range(2)]
                sqb = [buf(f"ad_sq{j}") for j in range(2)]
                r1b = [buf(f"ad_r1{j}") for j in range(2)]
                resb = [buf(f"ad_res{j}") for j in range(2)]
                zb = [buf(f"ad_z{j}") for j in range(2)]
                out_tile = lambda c2, hh: s_actb[:, c2, hh * TH:(hh + 1) * TH]

                for d in range(CK):
                    par = d % 2
                    dma_in('lA' if par == 0 else 'lB', s_wsl[par][:],
                           wo_r[d].rearrange("p ck q -> p (ck q)"), [wslb[par]])
                    act([wslb[par]], [wslbb[par]],
                        lambda e, pa=par: e.activation(out=s_wslb[pa][:], in_=s_wsl[pa][:],
                                                       func=AF.Copy))
                    builders = []
                    for ck in range(CK):
                        st, sp = (ck == 0), (ck == CK - 1)
                        for hh in range(2):
                            builders.append(
                                lambda e, c=ck, st=st, sp=sp, pa=par, h2=hh:
                                e.matmul(pb[2 * pa + h2][:, :],
                                         s_wslb[pa][:, c * 128:(c + 1) * 128],
                                         out_tile(c, h2), start=st, stop=sp))
                    mm([wslbb[par]] + ACTB, [PB[2 * par], PB[2 * par + 1]], builders)
                    for hh in range(2):
                        pk = 2 * par + hh
                        act([PB[pk]], [yb[d]],
                            lambda e, k=pk, d2=d, h2=hh: e.activation(
                                out=s_y[:, d2, h2 * TH:(h2 + 1) * TH], in_=pb[k][:, :],
                                func=AF.Copy))
                    act([yb[d]], [cbb[par]],
                        lambda e, d2=d, pa=par: e.activation(out=s_cb[pa][:], in_=s_y[:, d2, :],
                                                             func=AF.Copy))
                    act([cbb[par]], [sqb[par]],
                        lambda e, pa=par: e.activation(out=s_sq[pa][:], in_=s_cb[pa][:],
                                                       func=AF.Square))
                    st, sp = (d == 0), (d == CK - 1)
                    mm([cbb[par], CONSTS], [PB[4], PB[5]],
                       [(lambda e, pa=par, st=st, sp=sp:
                         e.matmul(pb[4][0:1, :], s_ones_colb[:], s_cb[pa][:, 0:TH], start=st, stop=sp)),
                        (lambda e, pa=par, st=st, sp=sp:
                         e.matmul(pb[5][0:1, :], s_ones_colb[:], s_cb[pa][:, TH:T], start=st, stop=sp))])
                    mm([sqb[par], CONSTS], [PB[6], PB[7]],
                       [(lambda e, pa=par, st=st, sp=sp:
                         e.matmul(pb[6][0:1, :], s_ones_colb[:], s_sq[pa][:, 0:TH], start=st, stop=sp)),
                        (lambda e, pa=par, st=st, sp=sp:
                         e.matmul(pb[7][0:1, :], s_ones_colb[:], s_sq[pa][:, TH:T], start=st, stop=sp))])
                act([PB[4]], [RM],
                    lambda e: e.activation(out=s_rowM[0:1, 0:TH], in_=pb[4][0:1, :], func=AF.Copy,
                                           scale=1.0 / C))
                act([PB[5], RM], [RM],
                    lambda e: e.activation(out=s_rowM[0:1, TH:T], in_=pb[5][0:1, :], func=AF.Copy,
                                           scale=1.0 / C))
                act([PB[6]], [RE],
                    lambda e: e.activation(out=s_rowE[0:1, 0:TH], in_=pb[6][0:1, :], func=AF.Copy,
                                           scale=1.0 / C))
                act([PB[7], RE], [RE],
                    lambda e: e.activation(out=s_rowE[0:1, TH:T], in_=pb[7][0:1, :], func=AF.Copy,
                                           scale=1.0 / C))
                dve([RM], [RT],
                    lambda e: e.scalar_tensor_tensor(out=s_rowT[:], in0=s_rowM[:], scalar=1.0,
                                                     in1=s_rowM[:], op0=ALU.mult, op1=ALU.mult))
                dve([RE, RT], [RV],
                    lambda e: e.tensor_tensor(out=s_rowV[:], in0=s_rowE[:], in1=s_rowT[:],
                                              op=ALU.subtract))
                act([RV, CONSTS], [RT],
                    lambda e: e.activation(out=s_rowT[:], in_=s_rowV[:], func=AF.Sqrt,
                                           bias=s_epsln[0:1, :]))
                dve([RT], [RR],
                    lambda e: e.reciprocal(out=s_rowR[:], in_=s_rowT[:]))
                dve([RM, RR], [RV],
                    lambda e: e.scalar_tensor_tensor(out=s_rowV[:], in0=s_rowM[:], scalar=-1.0,
                                                     in1=s_rowR[:], op0=ALU.mult, op1=ALU.mult))
                bcast_pair(s_rowR, 0, 1, RR, s_bcA, BCA)
                bcast_pair(s_rowV, 2, 3, RV, s_bcB, BCB)
                for d in range(CK):
                    par = d % 2
                    dve([yb[d], BCA], [r1b[par]],
                        lambda e, d2=d, pa=par: e.tensor_tensor(
                            out=s_r1[pa][:], in0=s_y[:, d2, :], in1=s_bcA[:], op=ALU.mult))
                    dve([r1b[par], BCB], [zb[par]],
                        lambda e, pa=par: e.tensor_tensor(
                            out=s_z[pa][:], in0=s_r1[pa][:], in1=s_bcB[:], op=ALU.add))
                    dve([zb[par], CONSTS], [r1b[par]],
                        lambda e, pa=par, d2=d: e.tensor_scalar(
                            out=s_r1[pa][:], in0=s_z[pa][:], scalar1=s_lnw[:, d2:d2 + 1],
                            scalar2=s_lnb[:, d2:d2 + 1], op0=ALU.mult, op1=ALU.add))
                    dma_in('lE' if par == 0 else 'lF', s_res[par][:], res1_dram[d], [resb[par]],
                           rbufs=[buf(f"d_res1_{d}_0"), buf(f"d_res1_{d}_1")])
                    dve([r1b[par], resb[par]], [zb[par]],
                        lambda e, pa=par: e.tensor_tensor(
                            out=s_z[pa][:], in0=s_res[pa][:], in1=s_r1[pa][:], op=ALU.add))
                    dma_out('sC' if par == 0 else 'sD', res2_dram[d], s_z[par][:], [zb[par]],
                            wbufs=[buf(f"d_res2_{d}")])

        emit_ffn(0)
        emit_attn()
        emit_ffn(2)

        final_evs = [(s, p.cnt[s]) for s in SEMS if p.cnt[s] > 0]
        wl = p._waits('gpsimd', final_evs)
        p.streams['gpsimd'].append((wl, [lambda e: e.nop()], 'pool', 1))

        block = E(nc.Block())

        @block.sync
        def _(eng):
            p.realize(eng, 'sync', sems)

        @block.scalar
        def _(eng):
            p.realize(eng, 'scalar', sems)

        @block.vector
        def _(eng):
            p.realize(eng, 'vector', sems)

        @block.tensor
        def _(eng):
            p.realize(eng, 'tensor', sems)

        @block.gpsimd
        def _(eng):
            p.realize(eng, 'gpsimd', sems)

    return nc


_NC_CACHE = None


def _get_nc():
    global _NC_CACHE
    if _NC_CACHE is None:
        _NC_CACHE = build_nc()
    return _NC_CACHE


def kernel(L0_attn_x, L0_attn_Wq, L0_attn_Wk, L0_attn_Wv, L0_attn_Wo, L0_attn_conv_w,
           L0_attn_conv_b, L0_attn_gate_w, L0_attn_gate_b, L0_attn_ln_w, L0_attn_ln_b,
           L0_attn_gamma, L0_ffn_w_gate, L0_ffn_w_up, L0_ffn_w_down):
    f32 = np.float32
    x = np.ascontiguousarray(np.asarray(L0_attn_x, f32)).reshape(8192, 2048)

    def slab16(w):
        wt = np.ascontiguousarray(np.asarray(w, f32).T)
        return np.ascontiguousarray(wt.reshape(16, 128, 16, 128).transpose(2, 1, 0, 3))

    wg_t = np.ascontiguousarray(np.asarray(L0_ffn_w_gate, f32).T)
    wu_t = np.ascontiguousarray(np.asarray(L0_ffn_w_up, f32).T)
    wd_t = np.ascontiguousarray(np.asarray(L0_ffn_w_down, f32).T)
    wg_r = np.ascontiguousarray(wg_t.reshape(16, 128, 64, 128).transpose(2, 1, 0, 3))
    wu_r = np.ascontiguousarray(wu_t.reshape(16, 128, 64, 128).transpose(2, 1, 0, 3))
    wd_r = np.ascontiguousarray(wd_t.reshape(8, 8, 128, 16, 128).transpose(3, 0, 2, 1, 4))
    wk_r = slab16(L0_attn_Wk)
    wv_r = slab16(L0_attn_Wv)
    wq_r = slab16(L0_attn_Wq)
    wgt_r = slab16(L0_attn_gate_w)
    wo_r = slab16(L0_attn_Wo)

    def vec16(v):
        return np.ascontiguousarray(np.asarray(v, f32).reshape(16, 128).T)

    convw = np.ascontiguousarray(np.asarray(L0_attn_conv_w, f32)[:, 0, :]
                                 .reshape(16, 128, 4).transpose(1, 0, 2).reshape(128, 64))
    convb = vec16(L0_attn_conv_b)
    gateb = vec16(L0_attn_gate_b)
    lnw = vec16(L0_attn_ln_w)
    lnb = vec16(L0_attn_ln_b)
    gamma_row = np.ascontiguousarray(np.asarray(L0_attn_gamma, f32).reshape(1, 16))
    ones_row_ = np.ones((1, 128), f32)
    ones_col_ = np.ones((128, 1), f32)
    import ml_dtypes
    ones_colb_ = np.ones((128, 1), ml_dtypes.bfloat16)
    ident_ = np.eye(128, dtype=f32)
    iota_ = np.ascontiguousarray(np.broadcast_to(np.arange(1, T + 1, dtype=f32), (128, T)))

    nc = _get_nc()
    in_maps = []
    for c in range(NC):
        selv = np.zeros((NC, 1), f32)
        if c % 2 == 1:
            selv[c - 1, 0] = 1.0
        xs = np.ascontiguousarray(x[c * 1024:(c + 1) * 1024].T)
        in_maps.append(dict(
            x_fm=xs, wg_r=wg_r, wu_r=wu_r, wd_r=wd_r, wk_r=wk_r, wv_r=wv_r, wq_r=wq_r,
            wgt_r=wgt_r, wo_r=wo_r,
            wg_sl=np.ascontiguousarray(wg_r[8 * c:8 * c + 8]),
            wu_sl=np.ascontiguousarray(wu_r[8 * c:8 * c + 8]),
            wd_sl=np.ascontiguousarray(wd_r[2 * c:2 * c + 2]),
            convw=convw, convb=convb, gateb=gateb, lnw=lnw, lnb=lnb,
            gamma_row=gamma_row, sel=selv, ones_row=ones_row_, ones_col=ones_col_,
            ones_colb=ones_colb_,
            ident=ident_, iota_t=iota_,
        ))
    res = run_bass_kernel_spmd(nc, in_maps, core_ids=list(range(NC)))
    out = np.empty((8192, 2048), f32)
    for c in range(NC):
        out[c * 1024:(c + 1) * 1024] = res.results[c]["y_fm"].T
    return out.reshape(4, 2048, 2048)


# version 19
# speedup vs baseline: 1.0710x; 1.0710x over previous
import os
import numpy as np
from contextlib import ExitStack

import concourse.bass as bass
import concourse.mybir as mybir
from concourse.bass_utils import run_bass_kernel_spmd

F32 = mybir.dt.float32
BF16 = mybir.dt.bfloat16
I32 = mybir.dt.int32
AF = mybir.ActivationFunctionType
ALU = mybir.AluOpType
AX = mybir.AxisListType

T = 1024
TH = 512
C = 2048
CK = 16
II = 8192
IK = 64
H = 16
NC = 8
EPS_Q = 1e-5
EPS_LN = 1e-5
EPS_NRM = 1e-12

SEMS = ['bar', 'pe', 'act', 'dve', 'pool',
        'lA', 'lB', 'lC', 'lD', 'lE', 'lF', 'lG', 'lH', 'lI', 'lJ',
        'tq', 'sA', 'sB', 'sC', 'sD', 'cc']


class Buf:
    __slots__ = ('w', 'r', 'const')

    def __init__(self, const=False):
        self.w = None
        self.r = {}
        self.const = const


class Prog:
    def __init__(self):
        self.streams = {k: [] for k in ('sync', 'scalar', 'vector', 'tensor', 'gpsimd')}
        self.cnt = {s: 0 for s in SEMS}
        self.waited = {}
        self.B = {}

    def buf(self, name, const=False):
        if name not in self.B:
            self.B[name] = Buf(const)
        return self.B[name]

    def _waits(self, engk, evs):
        out = []
        for ev in evs:
            if ev is None:
                continue
            s, v = ev
            key = (engk, s)
            if v <= self.waited.get(key, 0):
                continue
            self.waited[key] = v
            out.append((s, v))
        return out

    def op(self, engk, sem, reads, writes, build, amt=1, group=None, extra_evs=()):
        evs = list(extra_evs)
        if amt == 16:
            evs.append((sem, self.cnt[sem]))
        for b in reads:
            evs.append(b.w)
        for b in writes:
            evs.append(b.w)
            evs.extend(b.r.values())
        wl = self._waits(engk, evs)
        self.cnt[sem] += amt
        ev = (sem, self.cnt[sem])
        self.streams[engk].append((wl, group if group is not None else [build], sem, amt))
        for b in reads:
            if not b.const:
                b.r[engk] = ev
        for b in writes:
            b.w = ev
            b.r = {}
        return ev

    def realize(self, eng, engk, sems):
        for wl, builders, sem, amt in self.streams[engk]:
            for s, v in wl:
                eng.wait_ge(sems[s], v)
            ins = None
            for b in builders:
                ins = b(eng)
            ins.then_inc(sems[sem], amt)


def build_nc():
    nc = bass.Bass("TRN2", target_bir_lowering=False)
    p = Prog()
    buf = p.buf

    def inp(name, shape, dt=F32):
        return nc.dram_tensor(name, shape, dt, kind="ExternalInput").ap()

    x_ext = inp("x_fm", [C, T])
    wg_r = inp("wg_r", [IK, 128, CK, 128])
    wu_r = inp("wu_r", [IK, 128, CK, 128])
    wd_r = inp("wd_r", [CK, 8, 128, 8, 128])
    wk_r = inp("wk_r", [CK, 128, CK, 128])
    wv_r = inp("wv_r", [CK, 128, CK, 128])
    wq_r = inp("wq_r", [CK, 128, CK, 128])
    wgt_r = inp("wgt_r", [CK, 128, CK, 128])
    wo_r = inp("wo_r", [CK, 128, CK, 128])
    wg_sl = inp("wg_sl", [8, 128, CK, 128])
    wu_sl = inp("wu_sl", [8, 128, CK, 128])
    wd_sl = inp("wd_sl", [2, 8, 128, 8, 128])
    convw = inp("convw", [128, 64])
    convb = inp("convb", [128, CK])
    gateb = inp("gateb", [128, CK])
    lnw = inp("lnw", [128, CK])
    lnb = inp("lnb", [128, CK])
    gamma_row = inp("gamma_row", [1, H])
    sel = inp("sel", [NC, 1])
    ones_row = inp("ones_row", [1, 128])
    ones_col = inp("ones_col", [128, 1])
    ones_colb = inp("ones_colb", [128, 1], BF16)
    ident = inp("ident", [128, 128])
    iota_t = inp("iota_t", [128, T])

    y_ext = nc.dram_tensor("y_fm", [C, T], F32, kind="ExternalOutput").ap()

    h_dram = nc.dram_tensor("h_dram", [IK, 128, T], F32).ap()
    wgq_dram = nc.dram_tensor("wgq_dram", [IK, 128, C], BF16).ap()
    wuq_dram = nc.dram_tensor("wuq_dram", [IK, 128, C], BF16).ap()
    wdq_dram = nc.dram_tensor("wdq_dram", [CK, 8, 128, 1024], BF16).ap()
    _dbg = {"kind": "ExternalOutput"} if os.environ.get("KD_DEBUG_SCRATCH") else {}
    res1_dram = nc.dram_tensor("res1_dram", [CK, 128, T], F32, **_dbg).ap()
    res2_dram = nc.dram_tensor("res2_dram", [CK, 128, T], F32, **_dbg).ap()
    xcb_dram = nc.dram_tensor("xcb_dram", [CK, 128, T], BF16).ap()
    s2_dram = nc.dram_tensor("s2_dram", [CK, 128, T], F32).ap()
    cc_in = nc.dram_tensor("cc_in", [1, 8192], F32).ap()
    cc_out = nc.dram_tensor("cc_out", [NC, 8192], F32).ap()
    bnc = nc.dram_tensor("bnc", [1, 8192], F32).ap()

    with ExitStack() as root:
        E = root.enter_context
        sems = {s: E(nc.semaphore(s)) for s in SEMS}

        s_ones_row = E(nc.sbuf_tensor("c_ones_row", [1, 128], F32))
        s_ones_col = E(nc.sbuf_tensor("c_ones_col", [128, 1], F32))
        s_ones_colb = E(nc.sbuf_tensor("c_ones_colb", [128, 1], BF16))
        s_ident = E(nc.sbuf_tensor("c_ident", [128, 128], F32))
        s_iota = E(nc.sbuf_tensor("c_iota", [128, T], F32))
        s_convw = E(nc.sbuf_tensor("c_convw", [128, 64], F32))
        s_convb = E(nc.sbuf_tensor("c_convb", [128, CK], F32))
        s_gateb = E(nc.sbuf_tensor("c_gateb", [128, CK], F32))
        s_lnw = E(nc.sbuf_tensor("c_lnw", [128, CK], F32))
        s_lnb = E(nc.sbuf_tensor("c_lnb", [128, CK], F32))
        s_sel = E(nc.sbuf_tensor("c_sel", [NC, 1], F32))
        s_grow = E(nc.sbuf_tensor("c_grow", [1, H], F32))
        s_lngrow = E(nc.sbuf_tensor("c_lngrow", [1, H], F32))
        s_lnm = E(nc.sbuf_tensor("c_lnm", [128, H], F32))
        s_gcols = E(nc.sbuf_tensor("c_gcols", [128, H], F32))
        s_wsrow = E(nc.sbuf_tensor("c_wsrow", [1, 3], F32))
        s_rwsrow = E(nc.sbuf_tensor("c_rwsrow", [1, 3], F32))
        s_wscols = E(nc.sbuf_tensor("c_wscols", [128, 3], F32))
        s_rwscols = E(nc.sbuf_tensor("c_rwscols", [128, 3], F32))
        s_stage16 = E(nc.sbuf_tensor("c_stage16", [128, 16], F32))
        s_stage48 = E(nc.sbuf_tensor("c_stage48", [128, 48], F32))
        s_evac16 = E(nc.sbuf_tensor("c_evac16", [16, 128], F32))
        s_evac48 = E(nc.sbuf_tensor("c_evac48", [48, 128], F32))
        s_actb = E(nc.sbuf_tensor("c_actb", [128, CK, T], BF16))
        s_rowM = E(nc.sbuf_tensor("c_rowM", [1, T], F32))
        s_rowE = E(nc.sbuf_tensor("c_rowE", [1, T], F32))
        s_rowT = E(nc.sbuf_tensor("c_rowT", [1, T], F32))
        s_rowV = E(nc.sbuf_tensor("c_rowV", [1, T], F32))
        s_rowR = E(nc.sbuf_tensor("c_rowR", [1, T], F32))
        s_bcA = E(nc.sbuf_tensor("c_bcA", [128, T], F32))
        s_bcB = E(nc.sbuf_tensor("c_bcB", [128, T], F32))
        s_sin = E(nc.sbuf_tensor("c_sin", [128, H], F32))
        s_tailc = E(nc.sbuf_tensor("c_tailc", [128, 48], F32))
        s_head6 = E(nc.sbuf_tensor("c_head6", [128, CK * 6], F32))
        s_epsln = E(nc.sbuf_tensor("c_epsln", [128, 1], F32))

        pb = [E(nc.psum_tensor(f"pb{k}", [128, 512], F32)) for k in range(8)]
        PB = [buf(f"pb{k}") for k in range(8)]
        CONSTS = buf("consts", const=True)

        def dma_in(sem, dst_ap, src_ap, wbufs, rbufs=(), extra_evs=()):
            return p.op('sync', sem, list(rbufs), list(wbufs),
                        lambda e, d=dst_ap, s=src_ap: e.dma_start(out=d, in_=s),
                        amt=16, extra_evs=extra_evs)

        def dma_out(sem, dst_ap, src_ap, rbufs, wbufs=(), extra_evs=()):
            return p.op('gpsimd', sem, list(rbufs), list(wbufs),
                        lambda e, d=dst_ap, s=src_ap: e.dma_start(out=d, in_=s),
                        amt=16, extra_evs=extra_evs)

        def act(reads, writes, build, extra_evs=()):
            return p.op('scalar', 'act', reads, writes, build, extra_evs=extra_evs)

        def dve(reads, writes, build, extra_evs=()):
            return p.op('vector', 'dve', reads, writes, build, extra_evs=extra_evs)

        def pool(reads, writes, build, extra_evs=()):
            return p.op('gpsimd', 'pool', reads, writes, build, extra_evs=extra_evs)

        def mm(reads, writes, builders, extra_evs=()):
            return p.op('tensor', 'pe', reads, writes, None, group=builders,
                        extra_evs=extra_evs)

        def tdma(reads, writes, build, extra_evs=()):
            return p.op('scalar', 'tq', reads, writes, build, amt=16, extra_evs=extra_evs)

        def barrier():
            evs = [(s2, p.cnt[s2]) for s2 in SEMS if p.cnt[s2] > 0]
            for engk in ('sync', 'scalar', 'vector', 'tensor', 'gpsimd'):
                wl = p._waits(engk, evs)
                p.cnt['bar'] += 1
                p.streams[engk].append((wl, [lambda e: e.nop()], 'bar', 1))

        def bcast_half(row_ap, bank, rowbuf):
            return mm([CONSTS, rowbuf], [PB[bank]],
                      [lambda e, r=row_ap, k=bank:
                       e.matmul(pb[k][:, :], s_ones_row[:], r, start=True, stop=True)])

        def bcast_pair(row_tensor, b0, b1, rowbuf, dst_tensor, dstbuf):
            bcast_half(row_tensor[0:1, 0:TH], b0, rowbuf)
            bcast_half(row_tensor[0:1, TH:T], b1, rowbuf)
            act([PB[b0]], [dstbuf],
                lambda e, k=b0: e.activation(out=dst_tensor[:, 0:TH], in_=pb[k][:, :], func=AF.Copy))
            act([PB[b1], dstbuf], [dstbuf],
                lambda e, k=b1: e.activation(out=dst_tensor[:, TH:T], in_=pb[k][:, :], func=AF.Copy))

        def row_dance(stagebuf, bank, outbuf, dst_row):
            mm([stagebuf, CONSTS], [PB[bank]],
               [lambda e, k=bank: e.transpose(pb[k][0:16, 0:128], s_stage16[:], s_ident[:])])
            evb = buf("evac16")
            act([PB[bank]], [evb],
                lambda e, k=bank: e.activation(out=s_evac16[:], in_=pb[k][0:16, 0:128], func=AF.Copy))
            bncb = buf("bnc")
            tdma([evb], [bncb],
                 lambda e: e.dma_start(out=bnc.rearrange("a (r c) -> (a r) c", c=128)[0:16, :],
                                       in_=s_evac16[:]))
            tdma([bncb], [outbuf],
                 lambda e, d=dst_row: e.dma_start(out=d[:], in_=bnc[:, 0:2048]))

        def dance32(fold_tensor, foldbuf, outbuf, dst_row):
            stg = buf("stage16")
            for j in range(16):
                bk = 4 + (j % 2)
                mm([foldbuf, CONSTS], [PB[bk]],
                   [lambda e, k=bk, j2=j, ft=fold_tensor:
                    e.transpose(pb[k][:, 0:128], ft[:, j2 * 128:(j2 + 1) * 128], s_ident[:])])
                dve([PB[bk]], [stg],
                    lambda e, k=bk, j2=j: e.tensor_reduce(
                        out=s_stage16[:, j2:j2 + 1], in_=pb[k][:, 0:128], axis=AX.X,
                        op=(ALU.max if j2 < 8 else ALU.min)))
            row_dance(stg, 6, outbuf, dst_row)

        def ln_stat_rows(rm, rv, rrstd):
            act([PB[0]], [rm],
                lambda e: e.activation(out=s_rowM[0:1, 0:TH], in_=pb[0][0:1, :], func=AF.Copy,
                                       scale=1.0 / C))
            act([PB[1], rm], [rm],
                lambda e: e.activation(out=s_rowM[0:1, TH:T], in_=pb[1][0:1, :], func=AF.Copy,
                                       scale=1.0 / C))
            re_ = buf("rowE")
            act([PB[2]], [re_],
                lambda e: e.activation(out=s_rowE[0:1, 0:TH], in_=pb[2][0:1, :], func=AF.Copy,
                                       scale=1.0 / C))
            act([PB[3], re_], [re_],
                lambda e: e.activation(out=s_rowE[0:1, TH:T], in_=pb[3][0:1, :], func=AF.Copy,
                                       scale=1.0 / C))
            rt = buf("rowT")
            dve([rm], [rt],
                lambda e: e.scalar_tensor_tensor(out=s_rowT[:], in0=s_rowM[:], scalar=1.0,
                                                 in1=s_rowM[:], op0=ALU.mult, op1=ALU.mult))
            dve([re_, rt], [rv],
                lambda e: e.tensor_tensor(out=s_rowV[:], in0=s_rowE[:], in1=s_rowT[:],
                                          op=ALU.subtract))
            act([rv, CONSTS], [rt],
                lambda e: e.activation(out=s_rowT[:], in_=s_rowV[:], func=AF.Sqrt,
                                       bias=s_epsln[0:1, :]))
            dve([rt], [rrstd],
                lambda e: e.reciprocal(out=s_rowR[:], in_=s_rowT[:]))

        for dst, src in [(s_ones_row, ones_row), (s_ones_col, ones_col),
                         (s_ones_colb, ones_colb), (s_ident, ident),
                         (s_iota, iota_t), (s_convw, convw), (s_convb, convb),
                         (s_gateb, gateb), (s_lnw, lnw), (s_lnb, lnb), (s_sel, sel),
                         (s_grow, gamma_row)]:
            dma_in('lA', dst[:], src, [CONSTS])

        p.op('vector', 'dve', [], [CONSTS], lambda e: e.memset(s_epsln[:], EPS_LN))
        act([CONSTS], [CONSTS],
            lambda e: e.activation(out=s_lngrow[:], in_=s_grow[:], func=AF.Ln))
        mm([CONSTS], [PB[0]],
           [lambda e: e.matmul(pb[0][:, 0:H], s_ones_row[:], s_lngrow[:], start=True, stop=True)])
        act([PB[0]], [CONSTS],
            lambda e: e.activation(out=s_lnm[:], in_=pb[0][:, 0:H], func=AF.Copy))
        act([CONSTS], [CONSTS],
            lambda e: e.activation(out=s_gcols[:], in_=s_lnm[:], func=AF.Exp))

        with ExitStack() as sc:
            E2 = sc.enter_context
            s_sl = [E2(nc.sbuf_tensor(f"p0_sl{j}", [128, 2048], F32)) for j in range(2)]
            s_par = [E2(nc.sbuf_tensor(f"p0_par{j}", [128, 1], F32)) for j in range(2)]
            s_acc = E2(nc.sbuf_tensor("p0_acc", [128, 3], F32))
            s_ccrow = E2(nc.sbuf_tensor("p0_ccrow", [1, 3], F32))
            slb = [buf(f"p0_sl{j}") for j in range(2)]
            parb = [buf(f"p0_par{j}") for j in range(2)]
            accb = buf("p0_acc")
            ccib = buf("d_ccin")
            ccob = buf("d_ccout")
            ccrb = buf("p0_ccrow")

            def slices_of(w_ap, widx):
                if widx < 2:
                    return [(w_ap[j].rearrange("p ck q -> p (ck q)"), 2048) for j in range(8)]
                return [(w_ap[a, b].rearrange("p ib q -> p (ib q)"), 1024)
                        for a in range(2) for b in range(8)]

            for widx, w_ap in enumerate([wg_sl, wu_sl, wd_sl]):
                srcs = slices_of(w_ap, widx)
                for j, (sap, ncols) in enumerate(srcs):
                    par = j % 2
                    dma_in('lB' if par == 0 else 'lC', s_sl[par][:, 0:ncols], sap, [slb[par]])
                    dve([slb[par]], [parb[par]],
                        lambda e, pa=par, n=ncols: e.tensor_reduce(
                            out=s_par[pa][:], in_=s_sl[pa][:, 0:n], axis=AX.X, op=ALU.add,
                            apply_absolute_value=True))
                    if j == 0:
                        pool([parb[par]], [accb],
                             lambda e, pa=par, w=widx: e.tensor_scalar(
                                 out=s_acc[:, w:w + 1], in0=s_par[pa][:], scalar1=1.0,
                                 scalar2=None, op0=ALU.mult))
                    else:
                        pool([parb[par], accb], [accb],
                             lambda e, pa=par, w=widx: e.tensor_tensor(
                                 out=s_acc[:, w:w + 1], in0=s_acc[:, w:w + 1],
                                 in1=s_par[pa][:], op=ALU.add))
            mm([accb, CONSTS], [PB[1]],
               [(lambda e, w=w: e.matmul(pb[1][0:1, w:w + 1], s_acc[:, w:w + 1],
                                         s_ones_col[:], start=True, stop=True)) for w in range(3)])
            act([PB[1]], [ccrb],
                lambda e: e.activation(out=s_ccrow[:], in_=pb[1][0:1, 0:3], func=AF.Copy))
            dma_out('sA', cc_in[:, 0:3], s_ccrow[:], [ccrb], wbufs=[ccib])
            p.op('gpsimd', 'cc', [ccib], [ccob],
                 lambda e: e.collective_compute(
                     "AllReduce", ALU.add, replica_groups=[list(range(NC))],
                     ins=[cc_in[:, 0:3]], outs=[cc_out[0:1, 0:3]]))
            dma_in('lD', s_ccrow[:], cc_out[0:1, 0:3], [ccrb], rbufs=[ccob])
            wsb = buf("wsrow")
            rwsb = buf("rwsrow")
            act([ccrb], [wsb],
                lambda e: e.activation(out=s_wsrow[:], in_=s_ccrow[:], func=AF.Copy,
                                       bias=EPS_Q, scale=1.0 / float(C * II)))
            dve([wsb], [rwsb],
                lambda e: e.reciprocal(out=s_rwsrow[:], in_=s_wsrow[:]))
            mm([wsb, CONSTS], [PB[2]],
               [lambda e: e.matmul(pb[2][:, 0:3], s_ones_row[:], s_wsrow[:], start=True, stop=True)])
            act([PB[2]], [buf("wscols")],
                lambda e: e.activation(out=s_wscols[:], in_=pb[2][:, 0:3], func=AF.Copy))
            mm([rwsb, CONSTS], [PB[3]],
               [lambda e: e.matmul(pb[3][:, 0:3], s_ones_row[:], s_rwsrow[:], start=True, stop=True)])
            act([PB[3]], [buf("rwscols")],
                lambda e: e.activation(out=s_rwscols[:], in_=pb[3][:, 0:3], func=AF.Copy))

        ACTB = [buf(f"actb{ck}") for ck in range(CK)]
        BCA = buf("bcA")
        BCB = buf("bcB")
        XSBC = buf("xsbc")
        SROW = buf("srow")
        RM = buf("rowM")
        RV = buf("rowV")
        RR = buf("rowR")
        RA = buf("rowA")
        RE = buf("rowE")
        RT = buf("rowT")
        RMX = buf("rowMX")

        def emit_ffn(li):
            first = (li == 0)
            ffn_stack = ExitStack()
            s_srow = ffn_stack.enter_context(nc.sbuf_tensor(f"f{li}_srow", [1, T], F32))
            s_xsbc = ffn_stack.enter_context(nc.sbuf_tensor(f"f{li}_xsbc", [128, T], F32))
            s_rowA = ffn_stack.enter_context(nc.sbuf_tensor(f"f{li}_rowA", [1, T], F32))
            s_rowMX = ffn_stack.enter_context(nc.sbuf_tensor(f"f{li}_rowMX", [1, 2048], F32))

            def x_chunk_src(ck):
                return x_ext[ck * 128:(ck + 1) * 128, :] if first else res2_dram[ck]

            x_dep = (lambda ck: []) if first else (lambda ck: [buf(f"d_res2_{ck}")])

            barrier()
            with ExitStack() as sA:
                E2 = sA.enter_context
                s_x = E2(nc.sbuf_tensor(f"f{li}_x", [128, CK, T], F32))
                s_sq = [E2(nc.sbuf_tensor(f"f{li}_sq{j}", [128, T], F32)) for j in range(2)]
                s_fold = [E2(nc.sbuf_tensor(f"f{li}_fold{j}", [128, 2048], F32)) for j in range(2)]
                s_t1 = [E2(nc.sbuf_tensor(f"f{li}_t1{j}", [128, T], F32)) for j in range(2)]
                s_i32 = [E2(nc.sbuf_tensor(f"f{li}_i32{j}", [128, T], I32)) for j in range(2)]
                xb = [buf(f"f{li}_x{ck}") for ck in range(CK)]
                sqb = [buf(f"f{li}_sq{j}") for j in range(2)]
                foldb = [buf(f"f{li}_fold{j}") for j in range(2)]
                t1b = [buf(f"f{li}_t1{j}") for j in range(2)]
                i32b = [buf(f"f{li}_i32{j}") for j in range(2)]

                ld_evs = []
                for ck in range(CK):
                    ld_evs.append(dma_in('lA' if ck % 2 == 0 else 'lB', s_x[:, ck, :],
                                         x_chunk_src(ck), [xb[ck]], rbufs=x_dep(ck)))
                x_all = (ld_evs[-1], ld_evs[-2])

                for ck in range(CK):
                    par = ck % 2
                    act([xb[ck]], [sqb[par]],
                        lambda e, c=ck, pa=par: e.activation(out=s_sq[pa][:], in_=s_x[:, c, :],
                                                             func=AF.Square))
                    st, sp = (ck == 0), (ck == CK - 1)
                    mm([xb[ck], CONSTS], [PB[0], PB[1]],
                       [(lambda e, c=ck, st=st, sp=sp:
                         e.matmul(pb[0][0:1, :], s_ones_col[:], s_x[:, c, 0:TH], start=st, stop=sp)),
                        (lambda e, c=ck, st=st, sp=sp:
                         e.matmul(pb[1][0:1, :], s_ones_col[:], s_x[:, c, TH:T], start=st, stop=sp))])
                    mm([sqb[par], CONSTS], [PB[2], PB[3]],
                       [(lambda e, pa=par, st=st, sp=sp:
                         e.matmul(pb[2][0:1, :], s_ones_col[:], s_sq[pa][:, 0:TH], start=st, stop=sp)),
                        (lambda e, pa=par, st=st, sp=sp:
                         e.matmul(pb[3][0:1, :], s_ones_col[:], s_sq[pa][:, TH:T], start=st, stop=sp))])
                    if ck == 0:
                        pool([xb[ck]], [foldb[0]],
                             lambda e, c=ck: e.tensor_scalar(
                                 out=s_fold[0][:, 0:1024], in0=s_x[:, c, :], scalar1=1.0,
                                 scalar2=None, op0=ALU.mult))
                        pool([xb[ck], foldb[0]], [foldb[0]],
                             lambda e, c=ck: e.tensor_scalar(
                                 out=s_fold[0][:, 1024:2048], in0=s_x[:, c, :], scalar1=1.0,
                                 scalar2=None, op0=ALU.mult))
                    else:
                        sfs, sfd = s_fold[(ck + 1) % 2], s_fold[ck % 2]
                        dve([xb[ck], foldb[(ck + 1) % 2]], [foldb[ck % 2]],
                            lambda e, c=ck, a=sfs, b=sfd: e.tensor_tensor(
                                out=b[:, 0:1024], in0=a[:, 0:1024], in1=s_x[:, c, :], op=ALU.max),)
                        dve([xb[ck], foldb[(ck + 1) % 2], foldb[ck % 2]], [foldb[ck % 2]],
                            lambda e, c=ck, a=sfs, b=sfd: e.tensor_tensor(
                                out=b[:, 1024:2048], in0=a[:, 1024:2048], in1=s_x[:, c, :], op=ALU.min))
                fin, sfin = foldb[(CK - 1) % 2], s_fold[(CK - 1) % 2]
                dance32(sfin, fin, RMX, s_rowMX)
                ln_stat_rows(RM, RV, RR)
                dve([RMX, RM], [RA],
                    lambda e: e.tensor_tensor(out=s_rowA[:], in0=s_rowMX[0:1, 0:T],
                                              in1=s_rowM[:], op=ALU.subtract))
                dve([RMX, RM], [RT],
                    lambda e: e.tensor_tensor(out=s_rowT[:], in0=s_rowM[:],
                                              in1=s_rowMX[0:1, T:2 * T], op=ALU.subtract))
                dve([RA, RT], [RE],
                    lambda e: e.tensor_tensor(out=s_rowE[:], in0=s_rowA[:], in1=s_rowT[:],
                                              op=ALU.max))
                dve([RE, RR], [RA],
                    lambda e: e.tensor_tensor(out=s_rowA[:], in0=s_rowE[:], in1=s_rowR[:],
                                              op=ALU.mult))
                act([RA], [SROW],
                    lambda e: e.activation(out=s_srow[:], in_=s_rowA[:], func=AF.Copy,
                                           scale=1.0 / 127.0, bias=EPS_Q))
                dve([SROW], [RA],
                    lambda e: e.reciprocal(out=s_rowA[:], in_=s_srow[:]))
                dve([RR, RA], [RT],
                    lambda e: e.tensor_tensor(out=s_rowT[:], in0=s_rowR[:], in1=s_rowA[:],
                                              op=ALU.mult))
                dve([RM, RT], [RV],
                    lambda e: e.scalar_tensor_tensor(out=s_rowV[:], in0=s_rowM[:], scalar=-1.0,
                                                     in1=s_rowT[:], op0=ALU.mult, op1=ALU.mult))
                bcast_pair(s_rowT, 4, 5, RT, s_bcA, BCA)
                bcast_pair(s_rowV, 6, 7, RV, s_bcB, BCB)
                bcast_pair(s_srow, 4, 5, SROW, s_xsbc, XSBC)
                for ck in range(CK):
                    par = ck % 2
                    dve([xb[ck], BCA], [t1b[par]],
                        lambda e, c=ck, pa=par: e.tensor_tensor(
                            out=s_t1[pa][:], in0=s_x[:, c, :], in1=s_bcA[:], op=ALU.mult))
                    dve([t1b[par], BCB], [i32b[par]],
                        lambda e, pa=par: e.tensor_tensor(
                            out=s_i32[pa][:], in0=s_t1[pa][:], in1=s_bcB[:], op=ALU.add))
                    act([i32b[par]], [ACTB[ck]],
                        lambda e, c=ck, pa=par: e.activation(out=s_actb[:, c, :],
                                                             in_=s_i32[pa][:], func=AF.Copy))

            barrier()
            with ExitStack() as sB:
                E2 = sB.enter_context
                s_wf = [E2(nc.sbuf_tensor(f"f{li}b_wf{j}", [128, C], F32)) for j in range(4)]
                s_wi = [E2(nc.sbuf_tensor(f"f{li}b_wi{j}", [128, C], I32)) for j in range(1)]
                s_gq = [E2(nc.sbuf_tensor(f"f{li}b_gq{j}", [128, C], BF16)) for j in range(4)]
                s_uq = [E2(nc.sbuf_tensor(f"f{li}b_uq{j}", [128, C], BF16)) for j in range(4)]
                s_gr = [E2(nc.sbuf_tensor(f"f{li}b_gr{j}", [128, TH], F32)) for j in range(4)]
                s_gs = [E2(nc.sbuf_tensor(f"f{li}b_gs{j}", [128, TH], F32)) for j in range(4)]
                s_h = [E2(nc.sbuf_tensor(f"f{li}b_h{j}", [128, T], F32)) for j in range(2)]
                s_hf = [E2(nc.sbuf_tensor(f"f{li}b_hf{j}", [128, 2048], F32)) for j in range(2)]
                wfb = [buf(f"f{li}b_wf{j}") for j in range(4)]
                wib = [buf(f"f{li}b_wi{j}") for j in range(1)]
                gqb = [buf(f"f{li}b_gq{j}") for j in range(4)]
                uqb = [buf(f"f{li}b_uq{j}") for j in range(4)]
                grb = [buf(f"f{li}b_gr{j}") for j in range(4)]
                gsb = [buf(f"f{li}b_gs{j}") for j in range(4)]
                hb = [buf(f"f{li}b_h{j}") for j in range(2)]
                hfb = [buf(f"f{li}b_hf{j}") for j in range(2)]

                def load_q(i, which):
                    par = i % 4
                    qp = i % 2
                    wsrc_r = wg_r if which == 'g' else wu_r
                    wq_d = wgq_dram if which == 'g' else wuq_dram
                    dstT = s_gq[par] if which == 'g' else s_uq[par]
                    dstB = gqb[par] if which == 'g' else uqb[par]
                    semld = {'g': ('lA', 'lB', 'lC', 'lD'),
                             'u': ('lE', 'lF', 'lG', 'lH')}[which][par]
                    dscr = buf(f"d_{'wgq' if which == 'g' else 'wuq'}_{i}")
                    if first:
                        rwi = 0 if which == 'g' else 1
                        dma_in(semld, s_wf[par][:], wsrc_r[i].rearrange("p ck q -> p (ck q)"),
                               [wfb[par]])
                        dve([wfb[par], buf("rwscols")], [wib[0]],
                            lambda e, pa=par, rw=rwi: e.tensor_scalar(
                                out=s_wi[0][:], in0=s_wf[pa][:], scalar1=s_rwscols[:, rw:rw + 1],
                                scalar2=None, op0=ALU.mult))
                        pool([wib[0]], [dstB],
                             lambda e, d=dstT: e.tensor_scalar(
                                 out=d[:], in0=s_wi[0][:], scalar1=1.0, scalar2=-1.0,
                                 op0=ALU.min, op1=ALU.max))
                        dma_out('sB', wq_d[i], dstT[:], [dstB], wbufs=[dscr])
                    else:
                        dma_in(semld, dstT[:], wq_d[i], [dstB], rbufs=[dscr])

                load_q(0, 'g')
                load_q(0, 'u')
                load_q(1, 'g')
                load_q(1, 'u')
                for i in range(IK):
                    par = i % 4
                    if i + 2 < IK:
                        load_q(i + 2, 'g')
                        load_q(i + 2, 'u')
                    qp = i % 2
                    for which in ('g', 'u'):
                        wqb_ = gqb[par] if which == 'g' else uqb[par]
                        wt = s_gq[par] if which == 'g' else s_uq[par]
                        off = 0 if which == 'g' else 2
                        builders = []
                        for ck in range(CK):
                            st, sp = (ck == 0), (ck == CK - 1)
                            for hh in range(2):
                                builders.append(
                                    lambda e, c=ck, st=st, sp=sp, pa=qp, w=wt, o=off, h2=hh:
                                    e.matmul(pb[4 * pa + o + h2][:, :],
                                             w[:, c * 128:(c + 1) * 128],
                                             s_actb[:, c, h2 * TH:(h2 + 1) * TH],
                                             start=st, stop=sp))
                        mm([wqb_] + ACTB, [PB[4 * qp + off], PB[4 * qp + off + 1]], builders)
                    for hh in range(2):
                        gi = 2 * qp + hh
                        dve([PB[4 * qp + hh], XSBC], [grb[gi]],
                            lambda e, pa=qp, h2=hh, g2=gi: e.scalar_tensor_tensor(
                                out=s_gr[g2][:], in0=pb[4 * pa + h2][:, :], scalar=1.0,
                                in1=s_xsbc[:, h2 * TH:(h2 + 1) * TH], op0=ALU.bypass, op1=ALU.mult))
                        act([grb[gi], buf("wscols")], [gsb[gi]],
                            lambda e, g2=gi: e.activation(out=s_gs[g2][:], in_=s_gr[g2][:],
                                                          func=AF.Silu, scale=s_wscols[:, 0:1]))
                        dve([PB[4 * qp + 2 + hh], gsb[gi]], [hb[qp]],
                            lambda e, pa=qp, h2=hh, g2=gi: e.tensor_tensor(
                                out=s_h[pa][:, h2 * TH:(h2 + 1) * TH], in0=s_gs[g2][:],
                                in1=pb[4 * pa + 2 + h2][:, :], op=ALU.mult))
                    if i == 0:
                        pool([hb[qp]], [hfb[0]],
                             lambda e, pa=qp: e.tensor_scalar(
                                 out=s_hf[0][:, 0:1024], in0=s_h[pa][:], scalar1=1.0,
                                 scalar2=None, op0=ALU.mult))
                        pool([hb[qp], hfb[0]], [hfb[0]],
                             lambda e, pa=qp: e.tensor_scalar(
                                 out=s_hf[0][:, 1024:2048], in0=s_h[pa][:], scalar1=1.0,
                                 scalar2=None, op0=ALU.mult))
                    else:
                        a2, b2 = s_hf[(i + 1) % 2], s_hf[i % 2]
                        dve([hb[qp], hfb[(i + 1) % 2]], [hfb[i % 2]],
                            lambda e, pa=qp, a=a2, b=b2: e.tensor_tensor(
                                out=b[:, 0:1024], in0=a[:, 0:1024], in1=s_h[pa][:], op=ALU.max))
                        dve([hb[qp], hfb[(i + 1) % 2], hfb[i % 2]], [hfb[i % 2]],
                            lambda e, pa=qp, a=a2, b=b2: e.tensor_tensor(
                                out=b[:, 1024:2048], in0=a[:, 1024:2048], in1=s_h[pa][:],
                                op=ALU.min))
                    dma_out('sC' if qp == 0 else 'sD', h_dram[i], s_h[qp][:], [hb[qp]],
                            wbufs=[buf(f"d_h_{i}")])
                hfin, shfin = hfb[(IK - 1) % 2], s_hf[(IK - 1) % 2]

                dance32(shfin, hfin, RMX, s_rowMX)
                dve([RMX], [RA],
                    lambda e: e.tensor_scalar(out=s_rowA[:], in0=s_rowMX[0:1, T:2 * T],
                                              scalar1=-1.0, scalar2=None, op0=ALU.mult))
                dve([RMX, RA], [RE],
                    lambda e: e.tensor_tensor(out=s_rowE[:], in0=s_rowMX[0:1, 0:T],
                                              in1=s_rowA[:], op=ALU.max))
                dve([SROW, buf("wsrow")], [RM],
                    lambda e: e.tensor_scalar(out=s_rowM[:], in0=s_srow[:],
                                              scalar1=s_wsrow[0:1, 1:2], scalar2=None,
                                              op0=ALU.mult))
                dve([RE, RM], [RT],
                    lambda e: e.tensor_tensor(out=s_rowT[:], in0=s_rowE[:], in1=s_rowM[:],
                                              op=ALU.mult))
                act([RT], [RV],
                    lambda e: e.activation(out=s_rowV[:], in_=s_rowT[:], func=AF.Copy,
                                           scale=1.0 / 127.0, bias=EPS_Q))
                dve([RV], [RA],
                    lambda e: e.reciprocal(out=s_rowA[:], in_=s_rowV[:]))
                dve([RA, RM], [RR],
                    lambda e: e.tensor_tensor(out=s_rowR[:], in0=s_rowA[:], in1=s_rowM[:],
                                              op=ALU.mult))
                dve([RV, buf("wsrow")], [RT],
                    lambda e: e.tensor_scalar(out=s_rowT[:], in0=s_rowV[:],
                                              scalar1=s_wsrow[0:1, 2:3], scalar2=0.5,
                                              op0=ALU.mult, op1=ALU.mult))
                bcast_pair(s_rowR, 4, 5, RR, s_bcA, BCA)
                bcast_pair(s_rowT, 6, 7, RT, s_bcB, BCB)

            barrier()
            with ExitStack() as sD:
                E2 = sD.enter_context
                s_hq = E2(nc.sbuf_tensor(f"f{li}d_hq", [128, IK, TH], BF16))
                s_hin = [E2(nc.sbuf_tensor(f"f{li}d_hin{j}", [128, TH], F32)) for j in range(4)]
                s_hi32 = [E2(nc.sbuf_tensor(f"f{li}d_hi32{j}", [128, TH], I32)) for j in range(2)]
                s_wdf = [E2(nc.sbuf_tensor(f"f{li}d_wdf{j}", [128, 1024], F32)) for j in range(4)]
                s_wdq = [E2(nc.sbuf_tensor(f"f{li}d_wdq{j}", [128, 1024], BF16)) for j in range(4)]
                s_wdi = E2(nc.sbuf_tensor(f"f{li}d_wdi", [128, 1024], I32))
                s_xc = [E2(nc.sbuf_tensor(f"f{li}d_xc{j}", [128, TH], F32)) for j in range(2)]
                s_y1 = [E2(nc.sbuf_tensor(f"f{li}d_y1{j}", [128, TH], F32)) for j in range(2)]
                s_y2 = [E2(nc.sbuf_tensor(f"f{li}d_y2{j}", [128, TH], F32)) for j in range(2)]
                hqb = [buf(f"f{li}d_hq{i}") for i in range(IK)]
                hinb = [buf(f"f{li}d_hin{j}") for j in range(4)]
                hi32b = [buf(f"f{li}d_hi32{j}") for j in range(2)]
                wdfb = [buf(f"f{li}d_wdf{j}") for j in range(4)]
                wdqb = [buf(f"f{li}d_wdq{j}") for j in range(4)]
                wdib = buf(f"f{li}d_wdi")
                xcb_ = [buf(f"f{li}d_xc{j}") for j in range(2)]
                y1b = [buf(f"f{li}d_y1{j}") for j in range(2)]
                y2b = [buf(f"f{li}d_y2{j}") for j in range(2)]

                for hh in range(2):
                    for i in range(IK):
                        par = i % 4
                        qp = i % 2
                        dma_in(('lA', 'lB', 'lC', 'lD')[par], s_hin[par][:],
                               h_dram[i][:, hh * TH:(hh + 1) * TH], [hinb[par]],
                               rbufs=[buf(f"d_h_{i}")])
                        dve([hinb[par], BCA], [hi32b[qp]],
                            lambda e, pa=par, q2=qp, h2=hh: e.tensor_tensor(
                                out=s_hi32[q2][:], in0=s_hin[pa][:],
                                in1=s_bcA[:, h2 * TH:(h2 + 1) * TH], op=ALU.mult))
                        act([hi32b[qp]], [hqb[i]],
                            lambda e, q2=qp, i2=i: e.activation(out=s_hq[:, i2, :],
                                                                in_=s_hi32[q2][:], func=AF.Copy))
                        if i % 8 == 7:
                            ig0 = i // 8
                            par2 = ig0 % 4
                            dscr0 = buf(f"d_wdq_0_{ig0}")
                            if first and hh == 0:
                                dma_in(('lE', 'lF', 'lG', 'lH')[par2], s_wdf[par2][:],
                                       wd_r[0, ig0].rearrange("p ib q -> p (ib q)"), [wdfb[par2]])
                                dve([wdfb[par2], buf("rwscols")], [wdib],
                                    lambda e, pa=par2: e.tensor_scalar(
                                        out=s_wdi[:], in0=s_wdf[pa][:],
                                        scalar1=s_rwscols[:, 2:3], scalar2=None, op0=ALU.mult))
                                pool([wdib], [wdqb[par2]],
                                     lambda e, pa=par2: e.tensor_scalar(
                                         out=s_wdq[pa][:], in0=s_wdi[:], scalar1=1.0,
                                         scalar2=-1.0, op0=ALU.min, op1=ALU.max))
                                dma_out('sB', wdq_dram[0, ig0], s_wdq[par2][:], [wdqb[par2]],
                                        wbufs=[dscr0])
                            else:
                                dma_in(('lE', 'lF', 'lG', 'lH')[par2], s_wdq[par2][:],
                                       wdq_dram[0, ig0], [wdqb[par2]], rbufs=[dscr0])
                            builders = []
                            for ib in range(8):
                                i3 = ig0 * 8 + ib
                                st, sp = (i3 == 0), (i3 == IK - 1)
                                builders.append(
                                    lambda e, pa=par2, ib2=ib, i4=i3, st=st, sp=sp:
                                    e.matmul(pb[0][:, :],
                                             s_wdq[pa][:, ib2 * 128:(ib2 + 1) * 128],
                                             s_hq[:, i4, :], start=st, stop=sp))
                            mm([wdqb[par2]] + [hqb[ig0 * 8 + ib] for ib in range(8)],
                               [PB[0]] if ig0 in (0, 7) else [], builders)
                    for d in [0]:
                        par = 0
                        if first:
                            xsrc = x_ext[d * 128:(d + 1) * 128, hh * TH:(hh + 1) * TH]
                            xdep = []
                        else:
                            xsrc = res2_dram[d][:, hh * TH:(hh + 1) * TH]
                            xdep = [buf(f"d_res2_{d}")]
                        dma_in('lI', s_xc[0][:], xsrc, [xcb_[0]], rbufs=xdep)
                        dve([PB[0], BCB], [y1b[0]],
                            lambda e, h2=hh: e.scalar_tensor_tensor(
                                out=s_y1[0][:], in0=pb[0][:, :], scalar=1.0,
                                in1=s_bcB[:, h2 * TH:(h2 + 1) * TH], op0=ALU.mult, op1=ALU.mult))
                        dve([y1b[0], xcb_[0]], [y2b[0]],
                            lambda e: e.tensor_tensor(
                                out=s_y2[0][:], in0=s_y1[0][:], in1=s_xc[0][:], op=ALU.add))
                        if first:
                            dst = res1_dram[d][:, hh * TH:(hh + 1) * TH]
                            wscr = [buf(f"d_res1_{d}_{hh}")]
                        else:
                            dst = y_ext[d * 128:(d + 1) * 128, hh * TH:(hh + 1) * TH]
                            wscr = []
                        dma_out('sA', dst, s_y2[0][:], [y2b[0]], wbufs=wscr)
                    for d in range(1, CK):
                        pbank = PB[d % 2]
                        for ig in range(8):
                            par = (d * 8 + ig) % 4
                            dscr = buf(f"d_wdq_{d}_{ig}")
                            if first and hh == 0:
                                dma_in(('lE', 'lF', 'lG', 'lH')[par], s_wdf[par][:],
                                       wd_r[d, ig].rearrange("p ib q -> p (ib q)"), [wdfb[par]])
                                dve([wdfb[par], buf("rwscols")], [wdib],
                                    lambda e, pa=par: e.tensor_scalar(
                                        out=s_wdi[:], in0=s_wdf[pa][:],
                                        scalar1=s_rwscols[:, 2:3], scalar2=None, op0=ALU.mult))
                                pool([wdib], [wdqb[par]],
                                     lambda e, pa=par: e.tensor_scalar(
                                         out=s_wdq[pa][:], in0=s_wdi[:], scalar1=1.0,
                                         scalar2=-1.0, op0=ALU.min, op1=ALU.max))
                                dma_out('sB', wdq_dram[d, ig], s_wdq[par][:], [wdqb[par]],
                                        wbufs=[dscr])
                            else:
                                dma_in(('lE', 'lF', 'lG', 'lH')[par], s_wdq[par][:],
                                       wdq_dram[d, ig], [wdqb[par]], rbufs=[dscr])
                            builders = []
                            for ib in range(8):
                                i2 = ig * 8 + ib
                                st, sp = (i2 == 0), (i2 == IK - 1)
                                builders.append(
                                    lambda e, pa=par, ib2=ib, i3=i2, st=st, sp=sp, d2=d:
                                    e.matmul(pb[d2 % 2][:, :],
                                             s_wdq[pa][:, ib2 * 128:(ib2 + 1) * 128],
                                             s_hq[:, i3, :], start=st, stop=sp))
                            mm([wdqb[par]] + [hqb[ig * 8 + ib] for ib in range(8)],
                               [pbank] if ig in (0, 7) else [], builders)
                        par = d % 2
                        if first:
                            xsrc = x_ext[d * 128:(d + 1) * 128, hh * TH:(hh + 1) * TH]
                            xdep = []
                        else:
                            xsrc = res2_dram[d][:, hh * TH:(hh + 1) * TH]
                            xdep = [buf(f"d_res2_{d}")]
                        dma_in('lI' if par == 0 else 'lJ', s_xc[par][:], xsrc, [xcb_[par]],
                               rbufs=xdep)
                        dve([pbank, BCB], [y1b[par]],
                            lambda e, pa=par, d2=d, h2=hh: e.scalar_tensor_tensor(
                                out=s_y1[pa][:], in0=pb[d2 % 2][:, :], scalar=1.0,
                                in1=s_bcB[:, h2 * TH:(h2 + 1) * TH], op0=ALU.mult, op1=ALU.mult))
                        dve([y1b[par], xcb_[par]], [y2b[par]],
                            lambda e, pa=par: e.tensor_tensor(
                                out=s_y2[pa][:], in0=s_y1[pa][:], in1=s_xc[pa][:], op=ALU.add))
                        if first:
                            dst = res1_dram[d][:, hh * TH:(hh + 1) * TH]
                            wscr = [buf(f"d_res1_{d}_{hh}")]
                        else:
                            dst = y_ext[d * 128:(d + 1) * 128, hh * TH:(hh + 1) * TH]
                            wscr = []
                        dma_out('sA', dst, s_y2[par][:], [y2b[par]], wbufs=wscr)
            ffn_stack.close()

        def emit_attn():
            barrier()
            with ExitStack() as sA:
                E2 = sA.enter_context
                s_x = E2(nc.sbuf_tensor("a0_x", [128, CK, T], F32))
                s_cb = [E2(nc.sbuf_tensor(f"a0_cb{j}", [128, T], BF16)) for j in range(2)]
                s_sq = [E2(nc.sbuf_tensor(f"a0_sq{j}", [128, T], BF16)) for j in range(2)]
                s_xp_ = [E2(nc.sbuf_tensor(f"a0_xp{j}", [128, T + 3], F32)) for j in range(2)]
                s_cv = [E2(nc.sbuf_tensor(f"a0_cv{j}", [128, T], F32)) for j in range(4)]
                s_cva = E2(nc.sbuf_tensor("a0_cva", [128, T], F32))
                s_cvb = E2(nc.sbuf_tensor("a0_cvb", [128, T], F32))
                s_xcb = E2(nc.sbuf_tensor("a0_xcb", [128, T], BF16))
                xb = [buf(f"a0_x{ck}") for ck in range(CK)]
                cbb = [buf(f"a0_cb{j}") for j in range(2)]
                sqb = [buf(f"a0_sq{j}") for j in range(2)]
                xpb_ = [buf(f"a0_xp{j}") for j in range(2)]
                cvb = [buf(f"a0_cv{j}") for j in range(4)]
                cvab = buf("a0_cva")
                cvbb = buf("a0_cvb")
                xcbb = buf("a0_xcb")

                ld_evs = []
                for ck in range(CK):
                    ld_evs.append(dma_in('lA' if ck % 2 == 0 else 'lB', s_x[:, ck, :],
                                         res1_dram[ck], [xb[ck]],
                                         rbufs=[buf(f"d_res1_{ck}_0"), buf(f"d_res1_{ck}_1")]))
                x_all = (ld_evs[-1], ld_evs[-2])
                for ck in range(CK):
                    par = ck % 2
                    act([xb[ck]], [cbb[par]],
                        lambda e, c=ck, pa=par: e.activation(out=s_cb[pa][:], in_=s_x[:, c, :],
                                                             func=AF.Copy))
                    act([cbb[par]], [sqb[par]],
                        lambda e, pa=par: e.activation(out=s_sq[pa][:], in_=s_cb[pa][:],
                                                       func=AF.Square))
                    st, sp = (ck == 0), (ck == CK - 1)
                    mm([cbb[par], CONSTS], [PB[0], PB[1]],
                       [(lambda e, pa=par, st=st, sp=sp:
                         e.matmul(pb[0][0:1, :], s_ones_colb[:], s_cb[pa][:, 0:TH], start=st, stop=sp)),
                        (lambda e, pa=par, st=st, sp=sp:
                         e.matmul(pb[1][0:1, :], s_ones_colb[:], s_cb[pa][:, TH:T], start=st, stop=sp))])
                    mm([sqb[par], CONSTS], [PB[2], PB[3]],
                       [(lambda e, pa=par, st=st, sp=sp:
                         e.matmul(pb[2][0:1, :], s_ones_colb[:], s_sq[pa][:, 0:TH], start=st, stop=sp)),
                        (lambda e, pa=par, st=st, sp=sp:
                         e.matmul(pb[3][0:1, :], s_ones_colb[:], s_sq[pa][:, TH:T], start=st, stop=sp))])
                ln_stat_rows(RM, RV, RR)
                dve([RM, RR], [RV],
                    lambda e: e.scalar_tensor_tensor(out=s_rowV[:], in0=s_rowM[:], scalar=-1.0,
                                                     in1=s_rowR[:], op0=ALU.mult, op1=ALU.mult))
                bcast_pair(s_rowR, 4, 5, RR, s_bcA, BCA)
                bcast_pair(s_rowV, 6, 7, RV, s_bcB, BCB)
                h6b = buf("head6")
                tstg = buf("tail48")
                for ck in range(CK):
                    par = ck % 2
                    s_xp = s_xp_[par]
                    xpb = xpb_[par]
                    dve([], [xpb], lambda e, sx=s_xp: e.memset(sx[:, 0:3], 0.0))
                    dve([xb[ck], BCA, xpb], [cvb[3]],
                        lambda e, c=ck: e.tensor_tensor(out=s_cv[3][:], in0=s_x[:, c, :],
                                                        in1=s_bcA[:], op=ALU.mult))
                    dve([cvb[3], BCB], [xpb],
                        lambda e, sx=s_xp: e.tensor_tensor(out=sx[:, 3:T + 3], in0=s_cv[3][:],
                                                           in1=s_bcB[:], op=ALU.add))
                    for j in range(4):
                        dve([xpb, CONSTS], [cvb[j]],
                            lambda e, c=ck, j2=j, sx=s_xp: e.tensor_scalar(
                                out=s_cv[j2][:], in0=sx[:, j2:T + j2],
                                scalar1=s_convw[:, 4 * c + j2:4 * c + j2 + 1],
                                scalar2=None, op0=ALU.mult))
                    pool([cvb[0], cvb[1]], [cvab],
                         lambda e: e.tensor_tensor(out=s_cva[:], in0=s_cv[0][:],
                                                   in1=s_cv[1][:], op=ALU.add))
                    pool([cvb[2], cvb[3]], [cvbb],
                         lambda e: e.tensor_tensor(out=s_cvb[:], in0=s_cv[2][:],
                                                   in1=s_cv[3][:], op=ALU.add))
                    pool([cvab, cvbb], [cvb[0]],
                         lambda e: e.tensor_tensor(out=s_cv[0][:], in0=s_cva[:],
                                                   in1=s_cvb[:], op=ALU.add))
                    act([cvb[0], CONSTS], [xcbb],
                        lambda e, c=ck: e.activation(out=s_xcb[:], in_=s_cv[0][:], func=AF.Silu,
                                                     bias=s_convb[:, c:c + 1]))
                    dma_out('sA', xcb_dram[ck], s_xcb[:], [xcbb], wbufs=[buf(f"d_xcb_{ck}")])
                    act([xpb], [ACTB[ck]],
                        lambda e, c=ck, sx=s_xp: e.activation(out=s_actb[:, c, :],
                                                              in_=sx[:, 3:T + 3], func=AF.Copy))
                    pool([xpb], [h6b],
                         lambda e, c=ck, sx=s_xp: e.tensor_scalar(
                             out=s_head6[:, 6 * c:6 * c + 6], in0=sx[:, 0:6], scalar1=1.0,
                             scalar2=None, op0=ALU.mult))
                    pool([xpb], [tstg],
                         lambda e, c=ck, sx=s_xp: e.tensor_scalar(
                             out=s_stage48[:, 3 * c:3 * c + 3], in0=sx[:, T:T + 3], scalar1=1.0,
                             scalar2=None, op0=ALU.mult))

            barrier()
            with ExitStack() as sB:
                E2 = sB.enter_context
                s_big1 = E2(nc.sbuf_tensor("a1_big1", [128, CK, T], F32))
                s_tmp = [E2(nc.sbuf_tensor(f"a1_tmp{j}", [128, T], F32)) for j in range(2)]
                big1 = [buf(f"a1_big1_{ck}") for ck in range(CK)]
                big2 = [buf(f"a1_big2_{ck}") for ck in range(CK)]
                tmpb = [buf(f"a1_tmp{j}") for j in range(2)]
                sub1 = ExitStack()
                E3 = sub1.enter_context
                s_wsl = [E3(nc.sbuf_tensor(f"a1_wsl{j}", [128, C], F32)) for j in range(3)]
                s_wslb = [E3(nc.sbuf_tensor(f"a1_wslb{j}", [128, C], BF16)) for j in range(3)]
                s_sqp = [E3(nc.sbuf_tensor(f"a1_sqp{j}", [128, TH], BF16)) for j in range(2)]
                s_rb = [E3(nc.sbuf_tensor(f"a1_rb{j}", [128, TH], F32)) for j in range(2)]
                s_nX = [E3(nc.sbuf_tensor(f"a1_nX{j}", [1, TH], F32)) for j in range(4)]
                s_nY = [E3(nc.sbuf_tensor(f"a1_nY{j}", [1, TH], F32)) for j in range(2)]
                s_nS = [E3(nc.sbuf_tensor(f"a1_nS{j}", [1, TH], F32)) for j in range(2)]
                s_kraw = [E3(nc.sbuf_tensor(f"a1_kraw{j}", [128, T], F32)) for j in range(2)]
                s_gt = s_kraw[1]
                wslb = [buf(f"a1_wsl{j}") for j in range(3)]
                wslbb = [buf(f"a1_wslb{j}") for j in range(3)]
                sqpb = [buf(f"a1_sqp{j}") for j in range(2)]
                rbb = [buf(f"a1_rb{j}") for j in range(2)]
                nXb = [buf(f"a1_nX{j}") for j in range(4)]
                nYb = [buf(f"a1_nY{j}") for j in range(2)]
                krawb = [buf(f"a1_kraw{j}") for j in range(2)]
                nSb = [buf(f"a1_nS{j}") for j in range(2)]
                gtb = buf("a1_kraw1")

                SW = {'wsl': s_wsl, 'wslb': s_wslb, 'sqp': s_sqp, 'rb': s_rb,
                      'nX': s_nX, 'nY': s_nY, 'nS': s_nS, 'kraw': s_kraw}

                def stream_w(w_ap, d, par):
                    nsw = len(SW['wsl'])
                    sp_ = d % nsw
                    wt, wbt = SW['wsl'][sp_], SW['wslb'][sp_]
                    dma_in(('lA', 'lB', 'lC', 'lD')[sp_ % 3], wt[:],
                           w_ap[d].rearrange("p ck q -> p (ck q)"), [wslb[sp_]])
                    act([wslb[sp_]], [wslbb[sp_]],
                        lambda e, a=wbt, b=wt: e.activation(out=a[:], in_=b[:], func=AF.Copy))
                    return sp_

                def proj_mms(d, par, rhs_bufs, rhs_tile, extra_evs=()):
                    sp_ = d % len(SW['wsl'])
                    wbt = SW['wslb'][sp_]
                    builders = []
                    for ck in range(CK):
                        st, sp = (ck == 0), (ck == CK - 1)
                        for hh in range(2):
                            builders.append(
                                lambda e, c=ck, st=st, sp=sp, pa=par, h2=hh, w=wbt:
                                e.matmul(pb[2 * pa + h2][:, :],
                                         w[:, c * 128:(c + 1) * 128],
                                         rhs_tile(c, h2), start=st, stop=sp))
                    mm([wslbb[sp_]] + rhs_bufs, [PB[2 * par], PB[2 * par + 1]], builders,
                       extra_evs=extra_evs)

                xlb_tile = lambda c, hh: s_actb[:, c, hh * TH:(hh + 1) * TH]

                def evac_raw(d, par):
                    kp = d % 2
                    krt = SW['kraw'][kp]
                    for hh in range(2):
                        pk = 2 * par + hh
                        act([PB[pk]], [krawb[kp]],
                            lambda e, k=pk, a=krt, h2=hh: e.activation(
                                out=a[:, h2 * TH:(h2 + 1) * TH], in_=pb[k][:, :], func=AF.Copy))

                def dance_a(d):
                    kp = d % 2
                    krt, sqt = SW['kraw'][kp], SW['sqp'][kp]
                    for hh in range(2):
                        np_ = 4 + kp
                        act([krawb[kp]], [sqpb[kp]],
                            lambda e, a=sqt, b=krt, h2=hh: e.activation(
                                out=a[:], in_=b[:, h2 * TH:(h2 + 1) * TH], func=AF.Square))
                        mm([sqpb[kp], CONSTS], [PB[np_]],
                           [lambda e, k=np_, a=sqt: e.matmul(pb[k][0:1, :], s_ones_colb[:],
                                                             a[:], start=True, stop=True)])
                        act([PB[np_]], [nXb[2 * kp + hh]],
                            lambda e, k=np_, a=SW['nX'][2 * kp + hh]: e.activation(
                                out=a[:], in_=pb[k][0:1, :], func=AF.Sqrt))

                def dance_b(d, dstT, dstbuf):
                    kp = d % 2
                    krt, rbt = SW['kraw'][kp], SW['rb'][kp]
                    for hh in range(2):
                        ni = 2 * kp + hh
                        nxt, nyt = SW['nX'][ni], SW['nY'][kp]
                        bb = 6 + kp
                        dve([nXb[ni]], [nYb[kp]],
                            lambda e, a=nyt, b=nxt: e.tensor_scalar(out=a[:], in0=b[:],
                                                                    scalar1=EPS_NRM, scalar2=None,
                                                                    op0=ALU.max))
                        dve([nYb[kp]], [nXb[ni]],
                            lambda e, a=nxt, b=nyt: e.reciprocal(out=a[:], in_=b[:]))
                        mm([nXb[ni], CONSTS], [PB[bb]],
                           [lambda e, k=bb, a=nxt: e.matmul(pb[k][:, :], s_ones_row[:],
                                                            a[:], start=True, stop=True)])
                        act([PB[bb]], [rbb[kp]],
                            lambda e, k=bb, a=rbt: e.activation(out=a[:],
                                                                in_=pb[k][:, :], func=AF.Copy))
                        dve([krawb[kp], rbb[kp]], [dstbuf],
                            lambda e, a=rbt, b=krt, d2=d, h2=hh: e.tensor_tensor(
                                out=dstT[:, d2, h2 * TH:(h2 + 1) * TH],
                                in0=b[:, h2 * TH:(h2 + 1) * TH], in1=a[:], op=ALU.mult))

                def proj_norm_loop(w_ap, dstT, dstbufs):
                    for d in range(CK):
                        par = d % 2
                        if d >= 2:
                            dance_b(d - 2, dstT, dstbufs[d - 2])
                        stream_w(w_ap, d, par)
                        proj_mms(d, par, ACTB, xlb_tile)
                        if d >= 1:
                            dance_a(d - 1)
                        evac_raw(d, par)
                    dance_a(CK - 1)
                    dance_b(CK - 2, dstT, dstbufs[CK - 2])
                    dance_b(CK - 1, dstT, dstbufs[CK - 1])

                proj_norm_loop(wk_r, s_big1, big1)
                for d in range(CK):
                    par = d % 2
                    stream_w(wv_r, d, par)
                    proj_mms(d, par, ACTB, xlb_tile)
                    for hh in range(2):
                        pk = 2 * par + hh
                        dve([PB[pk], big1[d]], [tmpb[par]],
                            lambda e, k=pk, pa=par, d2=d, h2=hh: e.tensor_tensor(
                                out=s_tmp[pa][:, h2 * TH:(h2 + 1) * TH], in0=pb[k][:, :],
                                in1=s_big1[:, d2, h2 * TH:(h2 + 1) * TH], op=ALU.mult))
                    pool([tmpb[par]], [big1[d]],
                         lambda e, pa=par, d2=d: e.tensor_copy(s_big1[:, d2, :], s_tmp[pa][:]))
                stg = buf("stage16")
                for hd in range(H):
                    par = hd % 2
                    dve([big1[hd], CONSTS], [gtb],
                        lambda e, h2=hd: e.tensor_scalar(out=s_gt[:], in0=s_big1[:, h2, :],
                                                         scalar1=0.0,
                                                         scalar2=s_gcols[:, h2:h2 + 1],
                                                         op0=ALU.mult, op1=ALU.add))
                    dve([gtb, big1[hd]], [tmpb[par]],
                        lambda e, h2=hd, pa=par: e.tensor_tensor_scan(
                            out=s_tmp[pa][:], data0=s_gt[:], data1=s_big1[:, h2, :],
                            initial=0.0, op0=ALU.mult, op1=ALU.add))
                    pool([tmpb[par]], [stg],
                         lambda e, pa=par, h2=hd: e.tensor_copy(
                             s_stage16[:, h2:h2 + 1], s_tmp[pa][:, T - 1:T]))
                    dma_out('sC' if par == 0 else 'sD', s2_dram[hd], s_tmp[par][:],
                            [tmpb[par]], wbufs=[big2[hd]])
                sub1.close()
                barrier()
                sub2 = ExitStack()
                E3 = sub2.enter_context
                mm([stg, CONSTS], [PB[0]],
                   [lambda e: e.transpose(pb[0][0:16, 0:128], s_stage16[:], s_ident[:])])
                ev16 = buf("evac16")
                act([PB[0]], [ev16],
                    lambda e: e.activation(out=s_evac16[:], in_=pb[0][0:16, 0:128], func=AF.Copy))
                ccib = buf("d_ccin2")
                dma_out('sA', cc_in.rearrange("a (r c) -> (a r) c", c=128)[0:16, :],
                        s_evac16[:], [ev16], wbufs=[ccib])
                tstg = buf("tail48")
                mm([tstg, CONSTS], [PB[1]],
                   [lambda e: e.transpose(pb[1][0:48, 0:128], s_stage48[:], s_ident[:])])
                ev48 = buf("evac48")
                act([PB[1]], [ev48],
                    lambda e: e.activation(out=s_evac48[:], in_=pb[1][0:48, 0:128], func=AF.Copy))
                dma_out('sA', cc_in.rearrange("a (r c) -> (a r) c", c=128)[16:64, :],
                        s_evac48[:], [ev48], wbufs=[ccib])
                ccog = buf("d_ccout2")
                p.op('gpsimd', 'cc', [ccib], [ccog],
                     lambda e: e.collective_compute(
                         "AllGather", ALU.bypass, replica_groups=[list(range(NC))],
                         ins=[cc_in[:]], outs=[cc_out[:]]))
                s_gat = [E3(nc.sbuf_tensor(f"a1_gat{j}", [NC, TH], F32)) for j in range(2)]
                s_piece = E3(nc.sbuf_tensor("a1_piece", [1, TH], F32))
                s_u16 = E3(nc.sbuf_tensor("a1_u16", [16, 128], F32))
                s_u48 = E3(nc.sbuf_tensor("a1_u48", [48, 128], F32))
                gatb = [buf(f"a1_gat{j}") for j in range(2)]
                pieceb = buf("a1_piece")
                u16b = buf("a1_u16")
                u48b = buf("a1_u48")
                bncb = buf("bnc")
                for j in range(16):
                    gpar = j % 2
                    dma_in('lE' if gpar == 0 else 'lF', s_gat[gpar][:],
                           cc_out[:, j * TH:(j + 1) * TH], [gatb[gpar]], rbufs=[ccog])
                    mm([gatb[gpar], CONSTS], [PB[2]],
                       [lambda e, gp=gpar: e.matmul(pb[2][0:1, :], s_sel[:],
                                                    s_gat[gp][:], start=True, stop=True)])
                    act([PB[2]], [pieceb],
                        lambda e: e.activation(out=s_piece[:], in_=pb[2][0:1, :], func=AF.Copy))
                    tdma([pieceb], [bncb],
                         lambda e, j2=j: e.dma_start(out=bnc[:, j2 * TH:(j2 + 1) * TH],
                                                     in_=s_piece[:]))
                dma_in('lE', s_u16[:], bnc.rearrange("a (r c) -> (a r) c", c=128)[0:16, :],
                       [u16b], rbufs=[bncb])
                dma_in('lF', s_u48[:], bnc.rearrange("a (r c) -> (a r) c", c=128)[16:64, :],
                       [u48b], rbufs=[bncb])
                mm([u16b, CONSTS], [PB[3]],
                   [lambda e: e.transpose(pb[3][:, 0:16], s_u16[:], s_ident[0:16, 0:16])])
                sinb = buf("sin")
                act([PB[3]], [sinb],
                    lambda e: e.activation(out=s_sin[:], in_=pb[3][:, 0:16], func=AF.Copy))
                mm([u48b, CONSTS], [PB[4]],
                   [lambda e: e.transpose(pb[4][:, 0:48], s_u48[:], s_ident[0:48, 0:48])])
                tailb = buf("tailc")
                act([PB[4]], [tailb],
                    lambda e: e.activation(out=s_tailc[:], in_=pb[4][:, 0:48], func=AF.Copy))
                s_fx = [E3(nc.sbuf_tensor(f"a1_fx{j}", [128, 3], F32)) for j in range(2)]
                s_fxb = E3(nc.sbuf_tensor("a1_fxb", [128, 3], BF16))
                fxb = [buf(f"a1_fx{j}") for j in range(2)]
                fxbb = buf("a1_fxb")
                h6b = buf("head6")
                for ck in range(CK):
                    pool([tailb], [h6b],
                         lambda e, c=ck: e.tensor_copy(s_head6[:, 6 * c:6 * c + 3],
                                                       s_tailc[:, 3 * c:3 * c + 3]))
                    dve([h6b, CONSTS], [fxb[0]],
                        lambda e, c=ck: e.tensor_scalar(
                            out=s_fx[0][:], in0=s_head6[:, 6 * c + 3:6 * c + 6],
                            scalar1=s_convw[:, 4 * c + 3:4 * c + 4], scalar2=None, op0=ALU.mult))
                    for j in range(3):
                        a2, b2 = s_fx[j % 2], s_fx[(j + 1) % 2]
                        dve([h6b, fxb[j % 2], CONSTS], [fxb[(j + 1) % 2]],
                            lambda e, c=ck, j2=j, a=a2, b=b2: e.scalar_tensor_tensor(
                                out=b[:], in0=s_head6[:, 6 * c + 2 - j2:6 * c + 5 - j2],
                                scalar=s_convw[:, 4 * c + 2 - j2:4 * c + 3 - j2],
                                in1=a[:], op0=ALU.mult, op1=ALU.add))
                    act([fxb[1], CONSTS], [fxbb],
                        lambda e, c=ck: e.activation(out=s_fxb[:], in_=s_fx[1][:], func=AF.Silu,
                                                     bias=s_convb[:, c:c + 1]))
                    dma_out('sB', xcb_dram[ck][:, 0:3], s_fxb[:], [fxbb],
                            wbufs=[buf(f"d_xcb_{ck}")])
                sub2.close()
                barrier()
                sub3 = ExitStack()
                E3 = sub3.enter_context
                s_wsl = [E3(nc.sbuf_tensor(f"a3_wsl{j}", [128, C], F32)) for j in range(3)]
                s_wslb = [E3(nc.sbuf_tensor(f"a3_wslb{j}", [128, C], BF16)) for j in range(3)]
                s_sqp = [E3(nc.sbuf_tensor(f"a3_sqp{j}", [128, TH], BF16)) for j in range(2)]
                s_rb = [E3(nc.sbuf_tensor(f"a3_rb{j}", [128, TH], F32)) for j in range(2)]
                s_nX = [E3(nc.sbuf_tensor(f"a3_nX{j}", [1, TH], F32)) for j in range(4)]
                s_nY = [E3(nc.sbuf_tensor(f"a3_nY{j}", [1, TH], F32)) for j in range(2)]
                s_nS = [E3(nc.sbuf_tensor(f"a3_nS{j}", [1, TH], F32)) for j in range(2)]
                s_kraw = [E3(nc.sbuf_tensor(f"a3_kraw{j}", [128, T], F32)) for j in range(2)]
                (SW['wsl'], SW['wslb'], SW['sqp'], SW['rb'], SW['nX'], SW['nY'],
                 SW['kraw'], SW['nS']) = (s_wsl, s_wslb, s_sqp, s_rb, s_nX, s_nY, s_kraw, s_nS)
                proj_norm_loop(wq_r, s_big1, big1)
                sub3.close()
                barrier()
                sub3b = ExitStack()
                E3 = sub3b.enter_context
                s_wsl = [E3(nc.sbuf_tensor(f"a3b_wsl{j}", [128, C], F32)) for j in range(2)]
                s_wslb = [E3(nc.sbuf_tensor(f"a3b_wslb{j}", [128, C], BF16)) for j in range(2)]
                SW['wsl'], SW['wslb'] = s_wsl, s_wslb
                s_xcb2 = E3(nc.sbuf_tensor("a1_xcb2", [128, CK, T], BF16))
                xcb2b = buf("a1_xcb2")
                xevs = []
                for ck in range(CK):
                    xevs.append(dma_in('lC' if ck % 2 == 0 else 'lD', s_xcb2[:, ck, :],
                                       xcb_dram[ck], [xcb2b], rbufs=[buf(f"d_xcb_{ck}")]))
                xcb_all = (xevs[-1], xevs[-2])
                s_gtmp = [E3(nc.sbuf_tensor(f"a1_gtmp{j}", [128, TH], F32)) for j in range(2)]
                gtmpb = [buf(f"a1_gtmp{j}") for j in range(2)]
                xcb_tile = lambda c2, hh: s_xcb2[:, c2, hh * TH:(hh + 1) * TH]
                for d in range(CK):
                    par = d % 2
                    stream_w(wgt_r, d, par)
                    proj_mms(d, par, [xcb2b], xcb_tile, extra_evs=xcb_all)
                    for hh in range(2):
                        pk = 2 * par + hh
                        act([PB[pk], CONSTS], [gtmpb[par]],
                            lambda e, k=pk, pa=par, d2=d: e.activation(
                                out=s_gtmp[pa][:], in_=pb[k][:, :], func=AF.Sigmoid,
                                bias=s_gateb[:, d2:d2 + 1]))
                        dve([gtmpb[par], big1[d]], [tmpb[par]],
                            lambda e, pa=par, d2=d, h2=hh: e.tensor_tensor(
                                out=s_tmp[pa][:, h2 * TH:(h2 + 1) * TH], in0=s_gtmp[pa][:],
                                in1=s_big1[:, d2, h2 * TH:(h2 + 1) * TH], op=ALU.mult))
                    pool([tmpb[par]], [big1[d]],
                         lambda e, pa=par, d2=d: e.tensor_copy(s_big1[:, d2, :], s_tmp[pa][:]))
                sub3b.close()
                barrier()
                sub4 = ExitStack()
                E3 = sub4.enter_context
                s_gp = E3(nc.sbuf_tensor("a1_gp", [128, T], F32))
                s_gp2 = E3(nc.sbuf_tensor("a1_gp2", [128, T], F32))
                s_sch = [E3(nc.sbuf_tensor(f"a1_sch{j}", [128, T], F32)) for j in range(2)]
                gpb = buf("a1_gp")
                gp2b = buf("a1_gp2")
                schb = [buf(f"a1_sch{j}") for j in range(2)]
                for hd in range(H):
                    par = hd % 2
                    dma_in('lA' if par == 0 else 'lB', s_sch[par][:], s2_dram[hd],
                           [schb[par]], rbufs=[big2[hd]])
                    act([CONSTS], [gpb],
                        lambda e, h2=hd: e.activation(out=s_gp[:], in_=s_iota[:], func=AF.Exp,
                                                      scale=s_lnm[:, h2:h2 + 1]))
                    dve([gpb, sinb], [gp2b],
                        lambda e, h2=hd: e.tensor_scalar(out=s_gp2[:], in0=s_gp[:],
                                                         scalar1=s_sin[:, h2:h2 + 1],
                                                         scalar2=None, op0=ALU.mult))
                    dve([gp2b, schb[par]], [tmpb[par]],
                        lambda e, pa=par: e.tensor_tensor(
                            out=s_tmp[pa][:], in0=s_sch[pa][:], in1=s_gp2[:], op=ALU.add))
                    dve([tmpb[par], big1[hd]], [ACTB[hd]],
                        lambda e, pa=par, h2=hd: e.tensor_tensor(
                            out=s_actb[:, h2, :], in0=s_big1[:, h2, :], in1=s_tmp[pa][:],
                            op=ALU.mult))
                sub4.close()

            barrier()
            with ExitStack() as sD:
                E2 = sD.enter_context
                s_wsl = [E2(nc.sbuf_tensor(f"ad_wsl{j}", [128, C], F32)) for j in range(2)]
                s_wslb = [E2(nc.sbuf_tensor(f"ad_wslb{j}", [128, C], BF16)) for j in range(2)]
                s_y = E2(nc.sbuf_tensor("ad_y", [128, CK, T], F32))
                s_cb = [E2(nc.sbuf_tensor(f"ad_cb{j}", [128, T], BF16)) for j in range(2)]
                s_sq = [E2(nc.sbuf_tensor(f"ad_sq{j}", [128, T], BF16)) for j in range(2)]
                s_r1 = [E2(nc.sbuf_tensor(f"ad_r1{j}", [128, T], F32)) for j in range(2)]
                s_res = [E2(nc.sbuf_tensor(f"ad_res{j}", [128, T], F32)) for j in range(2)]
                s_z = [E2(nc.sbuf_tensor(f"ad_z{j}", [128, T], F32)) for j in range(2)]
                wslb = [buf(f"ad_wsl{j}") for j in range(2)]
                wslbb = [buf(f"ad_wslb{j}") for j in range(2)]
                yb = [buf(f"ad_y{ck}") for ck in range(CK)]
                cbb = [buf(f"ad_cb{j}") for j in range(2)]
                sqb = [buf(f"ad_sq{j}") for j in range(2)]
                r1b = [buf(f"ad_r1{j}") for j in range(2)]
                resb = [buf(f"ad_res{j}") for j in range(2)]
                zb = [buf(f"ad_z{j}") for j in range(2)]
                out_tile = lambda c2, hh: s_actb[:, c2, hh * TH:(hh + 1) * TH]

                for d in range(CK):
                    par = d % 2
                    dma_in('lA' if par == 0 else 'lB', s_wsl[par][:],
                           wo_r[d].rearrange("p ck q -> p (ck q)"), [wslb[par]])
                    act([wslb[par]], [wslbb[par]],
                        lambda e, pa=par: e.activation(out=s_wslb[pa][:], in_=s_wsl[pa][:],
                                                       func=AF.Copy))
                    builders = []
                    for ck in range(CK):
                        st, sp = (ck == 0), (ck == CK - 1)
                        for hh in range(2):
                            builders.append(
                                lambda e, c=ck, st=st, sp=sp, pa=par, h2=hh:
                                e.matmul(pb[2 * pa + h2][:, :],
                                         s_wslb[pa][:, c * 128:(c + 1) * 128],
                                         out_tile(c, h2), start=st, stop=sp))
                    mm([wslbb[par]] + ACTB, [PB[2 * par], PB[2 * par + 1]], builders)
                    for hh in range(2):
                        pk = 2 * par + hh
                        act([PB[pk]], [yb[d]],
                            lambda e, k=pk, d2=d, h2=hh: e.activation(
                                out=s_y[:, d2, h2 * TH:(h2 + 1) * TH], in_=pb[k][:, :],
                                func=AF.Copy))
                    act([yb[d]], [cbb[par]],
                        lambda e, d2=d, pa=par: e.activation(out=s_cb[pa][:], in_=s_y[:, d2, :],
                                                             func=AF.Copy))
                    act([cbb[par]], [sqb[par]],
                        lambda e, pa=par: e.activation(out=s_sq[pa][:], in_=s_cb[pa][:],
                                                       func=AF.Square))
                    st, sp = (d == 0), (d == CK - 1)
                    mm([cbb[par], CONSTS], [PB[4], PB[5]],
                       [(lambda e, pa=par, st=st, sp=sp:
                         e.matmul(pb[4][0:1, :], s_ones_colb[:], s_cb[pa][:, 0:TH], start=st, stop=sp)),
                        (lambda e, pa=par, st=st, sp=sp:
                         e.matmul(pb[5][0:1, :], s_ones_colb[:], s_cb[pa][:, TH:T], start=st, stop=sp))])
                    mm([sqb[par], CONSTS], [PB[6], PB[7]],
                       [(lambda e, pa=par, st=st, sp=sp:
                         e.matmul(pb[6][0:1, :], s_ones_colb[:], s_sq[pa][:, 0:TH], start=st, stop=sp)),
                        (lambda e, pa=par, st=st, sp=sp:
                         e.matmul(pb[7][0:1, :], s_ones_colb[:], s_sq[pa][:, TH:T], start=st, stop=sp))])
                act([PB[4]], [RM],
                    lambda e: e.activation(out=s_rowM[0:1, 0:TH], in_=pb[4][0:1, :], func=AF.Copy,
                                           scale=1.0 / C))
                act([PB[5], RM], [RM],
                    lambda e: e.activation(out=s_rowM[0:1, TH:T], in_=pb[5][0:1, :], func=AF.Copy,
                                           scale=1.0 / C))
                act([PB[6]], [RE],
                    lambda e: e.activation(out=s_rowE[0:1, 0:TH], in_=pb[6][0:1, :], func=AF.Copy,
                                           scale=1.0 / C))
                act([PB[7], RE], [RE],
                    lambda e: e.activation(out=s_rowE[0:1, TH:T], in_=pb[7][0:1, :], func=AF.Copy,
                                           scale=1.0 / C))
                dve([RM], [RT],
                    lambda e: e.scalar_tensor_tensor(out=s_rowT[:], in0=s_rowM[:], scalar=1.0,
                                                     in1=s_rowM[:], op0=ALU.mult, op1=ALU.mult))
                dve([RE, RT], [RV],
                    lambda e: e.tensor_tensor(out=s_rowV[:], in0=s_rowE[:], in1=s_rowT[:],
                                              op=ALU.subtract))
                act([RV, CONSTS], [RT],
                    lambda e: e.activation(out=s_rowT[:], in_=s_rowV[:], func=AF.Sqrt,
                                           bias=s_epsln[0:1, :]))
                dve([RT], [RR],
                    lambda e: e.reciprocal(out=s_rowR[:], in_=s_rowT[:]))
                dve([RM, RR], [RV],
                    lambda e: e.scalar_tensor_tensor(out=s_rowV[:], in0=s_rowM[:], scalar=-1.0,
                                                     in1=s_rowR[:], op0=ALU.mult, op1=ALU.mult))
                bcast_pair(s_rowR, 0, 1, RR, s_bcA, BCA)
                bcast_pair(s_rowV, 2, 3, RV, s_bcB, BCB)
                for d in range(CK):
                    par = d % 2
                    dve([yb[d], BCA], [r1b[par]],
                        lambda e, d2=d, pa=par: e.tensor_tensor(
                            out=s_r1[pa][:], in0=s_y[:, d2, :], in1=s_bcA[:], op=ALU.mult))
                    dve([r1b[par], BCB], [zb[par]],
                        lambda e, pa=par: e.tensor_tensor(
                            out=s_z[pa][:], in0=s_r1[pa][:], in1=s_bcB[:], op=ALU.add))
                    dve([zb[par], CONSTS], [r1b[par]],
                        lambda e, pa=par, d2=d: e.tensor_scalar(
                            out=s_r1[pa][:], in0=s_z[pa][:], scalar1=s_lnw[:, d2:d2 + 1],
                            scalar2=s_lnb[:, d2:d2 + 1], op0=ALU.mult, op1=ALU.add))
                    dma_in('lE' if par == 0 else 'lF', s_res[par][:], res1_dram[d], [resb[par]],
                           rbufs=[buf(f"d_res1_{d}_0"), buf(f"d_res1_{d}_1")])
                    dve([r1b[par], resb[par]], [zb[par]],
                        lambda e, pa=par: e.tensor_tensor(
                            out=s_z[pa][:], in0=s_res[pa][:], in1=s_r1[pa][:], op=ALU.add))
                    dma_out('sC' if par == 0 else 'sD', res2_dram[d], s_z[par][:], [zb[par]],
                            wbufs=[buf(f"d_res2_{d}")])

        emit_ffn(0)
        emit_attn()
        emit_ffn(2)

        final_evs = [(s, p.cnt[s]) for s in SEMS if p.cnt[s] > 0]
        wl = p._waits('gpsimd', final_evs)
        p.streams['gpsimd'].append((wl, [lambda e: e.nop()], 'pool', 1))

        block = E(nc.Block())

        @block.sync
        def _(eng):
            p.realize(eng, 'sync', sems)

        @block.scalar
        def _(eng):
            p.realize(eng, 'scalar', sems)

        @block.vector
        def _(eng):
            p.realize(eng, 'vector', sems)

        @block.tensor
        def _(eng):
            p.realize(eng, 'tensor', sems)

        @block.gpsimd
        def _(eng):
            p.realize(eng, 'gpsimd', sems)

    return nc


_NC_CACHE = None


def _get_nc():
    global _NC_CACHE
    if _NC_CACHE is None:
        _NC_CACHE = build_nc()
    return _NC_CACHE


def kernel(L0_attn_x, L0_attn_Wq, L0_attn_Wk, L0_attn_Wv, L0_attn_Wo, L0_attn_conv_w,
           L0_attn_conv_b, L0_attn_gate_w, L0_attn_gate_b, L0_attn_ln_w, L0_attn_ln_b,
           L0_attn_gamma, L0_ffn_w_gate, L0_ffn_w_up, L0_ffn_w_down):
    f32 = np.float32
    x = np.ascontiguousarray(np.asarray(L0_attn_x, f32)).reshape(8192, 2048)

    def slab16(w):
        wt = np.ascontiguousarray(np.asarray(w, f32).T)
        return np.ascontiguousarray(wt.reshape(16, 128, 16, 128).transpose(2, 1, 0, 3))

    wg_t = np.ascontiguousarray(np.asarray(L0_ffn_w_gate, f32).T)
    wu_t = np.ascontiguousarray(np.asarray(L0_ffn_w_up, f32).T)
    wd_t = np.ascontiguousarray(np.asarray(L0_ffn_w_down, f32).T)
    wg_r = np.ascontiguousarray(wg_t.reshape(16, 128, 64, 128).transpose(2, 1, 0, 3))
    wu_r = np.ascontiguousarray(wu_t.reshape(16, 128, 64, 128).transpose(2, 1, 0, 3))
    wd_r = np.ascontiguousarray(wd_t.reshape(8, 8, 128, 16, 128).transpose(3, 0, 2, 1, 4))
    wk_r = slab16(L0_attn_Wk)
    wv_r = slab16(L0_attn_Wv)
    wq_r = slab16(L0_attn_Wq)
    wgt_r = slab16(L0_attn_gate_w)
    wo_r = slab16(L0_attn_Wo)

    def vec16(v):
        return np.ascontiguousarray(np.asarray(v, f32).reshape(16, 128).T)

    convw = np.ascontiguousarray(np.asarray(L0_attn_conv_w, f32)[:, 0, :]
                                 .reshape(16, 128, 4).transpose(1, 0, 2).reshape(128, 64))
    convb = vec16(L0_attn_conv_b)
    gateb = vec16(L0_attn_gate_b)
    lnw = vec16(L0_attn_ln_w)
    lnb = vec16(L0_attn_ln_b)
    gamma_row = np.ascontiguousarray(np.asarray(L0_attn_gamma, f32).reshape(1, 16))
    ones_row_ = np.ones((1, 128), f32)
    ones_col_ = np.ones((128, 1), f32)
    import ml_dtypes
    ones_colb_ = np.ones((128, 1), ml_dtypes.bfloat16)
    ident_ = np.eye(128, dtype=f32)
    iota_ = np.ascontiguousarray(np.broadcast_to(np.arange(1, T + 1, dtype=f32), (128, T)))

    nc = _get_nc()
    in_maps = []
    for c in range(NC):
        selv = np.zeros((NC, 1), f32)
        if c % 2 == 1:
            selv[c - 1, 0] = 1.0
        xs = np.ascontiguousarray(x[c * 1024:(c + 1) * 1024].T)
        in_maps.append(dict(
            x_fm=xs, wg_r=wg_r, wu_r=wu_r, wd_r=wd_r, wk_r=wk_r, wv_r=wv_r, wq_r=wq_r,
            wgt_r=wgt_r, wo_r=wo_r,
            wg_sl=np.ascontiguousarray(wg_r[8 * c:8 * c + 8]),
            wu_sl=np.ascontiguousarray(wu_r[8 * c:8 * c + 8]),
            wd_sl=np.ascontiguousarray(wd_r[2 * c:2 * c + 2]),
            convw=convw, convb=convb, gateb=gateb, lnw=lnw, lnb=lnb,
            gamma_row=gamma_row, sel=selv, ones_row=ones_row_, ones_col=ones_col_,
            ones_colb=ones_colb_,
            ident=ident_, iota_t=iota_,
        ))
    res = run_bass_kernel_spmd(nc, in_maps, core_ids=list(range(NC)))
    out = np.empty((8192, 2048), f32)
    for c in range(NC):
        out[c * 1024:(c + 1) * 1024] = res.results[c]["y_fm"].T
    return out.reshape(4, 2048, 2048)
